# version 10
# baseline (speedup 1.0000x reference)
import numpy as np
import ml_dtypes
import concourse.bass as bass
import concourse.mybir as mybir
from concourse.bass_utils import run_bass_kernel_spmd
from contextlib import ExitStack

F32 = mybir.dt.float32
BF16 = mybir.dt.bfloat16
ALU = mybir.AluOpType
AF = mybir.ActivationFunctionType
AXX = mybir.AxisListType.X

D = 1024
T = 2048
NT = 512
NN = T // NT
DC = D // 128
NLAYER = 8
EPS_OUT = 1e-5
EPS_IN = 1e-6
SCALE_B = 64.0 ** -0.5

ENGINES = ("pe", "act", "dve", "sp", "gp")
SECTION = {"pe": "tensor", "act": "scalar", "dve": "vector", "sp": "sync", "gp": "gpsimd"}
DEF_SEM = {"pe": "s_pe", "act": "s_act", "dve": "s_dve", "sp": None, "gp": None}
DEF_INC = {"pe": 1, "act": 1, "dve": 1, "sp": 16, "gp": 16}
SEM_NAMES = ("s_pe", "s_act", "s_dve",
             "s_sp0", "s_sp1", "s_sp2", "s_sp3",
             "s_gpd0", "s_gpd1", "s_gpd2", "s_gpd3",
             "s_gpo", "s_gcc")
TAP_ENABLE = set()


def nsl(n, w=NT):
    return slice(n * w, (n + 1) * w)



def _colblocks(W):
    din, dout = W.shape
    A = W.reshape(din // 128, 128, dout // 128, 128)
    return np.ascontiguousarray(A.transpose(2, 1, 0, 3).reshape(dout // 128, 128, din))


def _pack_attn_a(inp):
    aws, cbs = [], []
    for i in range(NLAYER):
        Wq = inp["L0_attn_a_Wq"][i].transpose(1, 0, 2).reshape(D, D)
        Wk = inp["L0_attn_a_Wk"][i].transpose(1, 0, 2).reshape(D, D)
        Wv = inp["L0_attn_a_Wv"][i].transpose(1, 0, 2).reshape(D, D)
        bq = _colblocks(np.ascontiguousarray(Wq))
        bk = _colblocks(np.ascontiguousarray(Wk))
        bv = _colblocks(np.ascontiguousarray(Wv))
        bo = _colblocks(inp["L0_attn_a_Wo"][i])
        b1 = _colblocks(inp["L0_attn_a_W1"][i])
        b2 = _colblocks(inp["L0_attn_a_W2"][i])
        units = []
        for c in range(DC):
            units.append(np.concatenate([bk[c], bv[c]], axis=1))
        for j in range(4):
            units.append(np.concatenate([bq[2 * j], bq[2 * j + 1]], axis=1))
        for j in range(4):
            units.append(np.concatenate([bo[2 * j], bo[2 * j + 1]], axis=1))
        for j in range(16):
            units.append(np.concatenate([b1[2 * j], b1[2 * j + 1]], axis=1))
        for m in range(DC):
            units.append(b2[m][:, 0:2048])
            units.append(b2[m][:, 2048:4096])
        aws.append(np.stack(units))

        def chunks(v):
            return np.ascontiguousarray(v.reshape(-1, 128).T)
        cb = np.concatenate([
            chunks(inp["L0_attn_a_bq"][i].reshape(-1)),
            chunks(inp["L0_attn_a_bk"][i].reshape(-1)),
            chunks(inp["L0_attn_a_bv"][i].reshape(-1)),
            chunks(inp["L0_attn_a_bo"][i]),
            chunks(inp["L0_attn_a_b1"][i]),
            chunks(inp["L0_attn_a_b2"][i]),
            chunks(inp["L0_attn_a_ln1_s"][i]),
            chunks(inp["L0_attn_a_ln1_b"][i]),
            chunks(inp["L0_attn_a_ln2_s"][i]),
            chunks(inp["L0_attn_a_ln2_b"][i]),
        ], axis=1)
        cbs.append(cb)
    return np.ascontiguousarray(np.stack(aws)), np.ascontiguousarray(np.stack(cbs))


def _pack_ffn(inp):
    out = []
    for e in range(9):
        if e == 0:
            Wg, Wu, Wd = inp["L0_ffn_Wg_s"], inp["L0_ffn_Wu_s"], inp["L0_ffn_Wd_s"]
        else:
            Wg, Wu, Wd = inp["L0_ffn_Wg"][e - 1], inp["L0_ffn_Wu"][e - 1], inp["L0_ffn_Wd"][e - 1]
        bg = _colblocks(np.ascontiguousarray(Wg.T))
        bu = _colblocks(np.ascontiguousarray(Wu.T))
        bd = _colblocks(np.ascontiguousarray(Wd.T))
        units = [np.concatenate([bg[m], bu[m]], axis=1) for m in range(16)]
        units += [bd[m] for m in range(DC)]
        out.append(np.stack(units))
    return np.ascontiguousarray(np.stack(out))


def _pack_attn_b(inp):
    bq = _colblocks(inp["L0_attn_b_W_qkv"])
    bo = _colblocks(inp["L0_attn_b_W_out"])
    units = [np.concatenate([bq[2 * j], bq[2 * j + 1]], axis=1) for j in range(12)]
    units += [np.concatenate([bo[2 * j], bo[2 * j + 1]], axis=1) for j in range(4)]
    return np.ascontiguousarray(np.stack(units))


def _host_consts(core):
    hd = 64
    t0 = (core % 2) * T
    inv_freq = 1.0 / (500000.0 ** (np.arange(0, hd, 2, dtype=np.float64) / hd))
    t = np.arange(t0, t0 + T, dtype=np.float64)
    emb = np.concatenate([t[:, None] * inv_freq[None, :]] * 2, axis=1)
    cos = np.cos(emb).astype(np.float32).T
    sin = np.sin(emb).astype(np.float32).T
    cos2 = np.ascontiguousarray(np.concatenate([cos, cos], axis=0))
    sin2 = np.ascontiguousarray(np.concatenate([sin, sin], axis=0))
    perm = np.zeros((128, 128), np.float32)
    for o in (0, 64):
        for j in range(32):
            perm[o + j + 32, o + j] = -1.0
            perm[o + j, o + j + 32] = 1.0
    ident = np.eye(128, dtype=np.float32)
    return dict(cost=cos2, sint=sin2, perm=perm, ident=ident,
                ones1=np.ones((1, 128), np.float32),
                onesb=np.ones((128, 1), ml_dtypes.bfloat16),
                onesw=np.ones((128, 64), ml_dtypes.bfloat16))



class Prog:
    def __init__(self, nc, block, sems):
        self.nc, self.block, self.sems = nc, block, sems
        self.q = {e: [] for e in ENGINES}
        self.cnt = {s: 0 for s in SEM_NAMES}
        self.waited = {e: {s: 0 for s in SEM_NAMES} for e in ENGINES}
        self.tiles = {}
        self.dma_n = {"sp": 0, "gp": 0}

    def _need(self, reads, writes):
        m = {}

        def add(tok):
            if tok is not None:
                s, v = tok
                if m.get(s, 0) < v:
                    m[s] = v
        for k in reads:
            t = self.tiles.get(k)
            if t:
                add(t[0])
        for k in writes:
            t = self.tiles.get(k)
            if t:
                add(t[0])
                for r in t[1]:
                    add(r)
        return m

    def emit_dma(self, eng, fn, reads=(), writes=()):
        lane = self.dma_n[eng] % 4
        self.dma_n[eng] += 1
        base = "s_sp" if eng == "sp" else "s_gpd"
        return self.emit(eng, fn, reads=tuple(reads) + (("dmalane", eng, lane),),
                         writes=tuple(writes) + (("dmalane", eng, lane),),
                         sem=f"{base}{lane}", inc=16)

    def emit(self, eng, fns, reads=(), writes=(), sem=None, inc=None):
        sem = sem or DEF_SEM[eng]
        assert sem is not None, "DMA queues must go through emit_dma"
        if inc is None:
            inc = DEF_INC[eng]
        need = self._need(reads, writes)
        waits = []
        for s, v in need.items():
            if self.waited[eng][s] < v:
                waits.append((s, v))
                self.waited[eng][s] = v
        if not isinstance(fns, (list, tuple)):
            fns = [fns]
        tok = None
        if inc and fns:
            self.cnt[sem] += inc
            tok = (sem, self.cnt[sem])
            for k in writes:
                self.tiles[k] = (tok, [])
            for k in reads:
                self.tiles.setdefault(k, (None, []))[1].append(tok)
        self.q[eng].append((waits, fns, sem, inc))
        return tok

    def wait_only(self, eng, keys):
        need = self._need(keys, ())
        waits = []
        for s, v in need.items():
            if self.waited[eng][s] < v:
                waits.append((s, v))
                self.waited[eng][s] = v
        if waits:
            self.q[eng].append((waits, [], None, 0))

    def barrier(self):
        toks = [(s, self.cnt[s]) for s in SEM_NAMES if self.cnt[s] > 0]
        for e in ENGINES:
            waits = []
            for s, v in toks:
                if self.waited[e][s] < v:
                    waits.append((s, v))
                    self.waited[e][s] = v
            if waits:
                self.q[e].append((waits, [], None, 0))

    def flush(self):
        for eng in ENGINES:
            steps = self.q[eng]
            if not steps:
                continue
            self.q[eng] = []
            sems = self.sems

            def section(e, steps=steps, sems=sems):
                for waits, fns, sem, inc in steps:
                    for s, v in waits:
                        e.wait_ge(sems[s], v)
                    last = None
                    for fn in fns:
                        last = fn(e)
                    if inc and last is not None:
                        last.then_inc(sems[sem], inc)
            getattr(self.block, SECTION[eng])(section)



def build_program(upto="all"):
    nc = bass.Bass("TRN2", target_bir_lowering=False, debug=False)

    xin = nc.dram_tensor("x", [D, T], F32, kind="ExternalInput").ap()
    aw = nc.dram_tensor("aw", [NLAYER, 48, 128, 2048], F32, kind="ExternalInput").ap()
    fw = nc.dram_tensor("fw", [9, 24, 128, 2048], F32, kind="ExternalInput").ap()
    bw = nc.dram_tensor("bw", [16, 128, 2048], F32, kind="ExternalInput").ap()
    cbd = nc.dram_tensor("cb", [NLAYER, 128, 104], F32, kind="ExternalInput").ap()
    wrd = nc.dram_tensor("wr", [128, 64], F32, kind="ExternalInput").ap()
    rbd = nc.dram_tensor("rbt", [128, 8], F32, kind="ExternalInput").ap()
    cosd = nc.dram_tensor("cost", [128, T], F32, kind="ExternalInput").ap()
    sind = nc.dram_tensor("sint", [128, T], F32, kind="ExternalInput").ap()
    permd = nc.dram_tensor("perm", [128, 128], F32, kind="ExternalInput").ap()
    identd = nc.dram_tensor("ident", [128, 128], F32, kind="ExternalInput").ap()
    ones1d = nc.dram_tensor("ones1", [1, 128], F32, kind="ExternalInput").ap()
    onesbd = nc.dram_tensor("onesb", [128, 1], BF16, kind="ExternalInput").ap()
    oneswd = nc.dram_tensor("onesw", [128, 64], BF16, kind="ExternalInput").ap()
    outd = nc.dram_tensor("out", [D, T], F32, kind="ExternalOutput").ap()

    cc1i = nc.dram_tensor("cc1i", [128, 16], F32, kind="Internal")
    cc1o = nc.dram_tensor("cc1o", [128, 16], F32, kind="Internal")
    cwrows = nc.dram_tensor("cwrows", [8, T], F32, kind="Internal")
    cc2i = nc.dram_tensor("cc2i", [64, 1040], F32, kind="Internal")
    cc2o = nc.dram_tensor("cc2o", [64, 1040], F32, kind="Internal")

    with ExitStack() as stk:
        def sb(name, shape, dt):
            return stk.enter_context(nc.sbuf_tensor(name, shape, dt))

        X = [sb(f"X{c}", [128, T], F32) for c in range(DC)]
        XBF = [sb(f"XBF{c}", [128, T], BF16) for c in range(DC)]
        QF = [sb(f"QF{c}", [128, T], BF16) for c in range(DC)]
        SCR = [sb(f"SCR{j}", [128, T], F32) for j in range(3)]
        POOLB = sb("POOLB", [128, 8 * T], BF16)
        RING = [sb(f"RING{j}", [128, 2048], BF16) for j in range(3)]
        SMALL = sb("SMALL", [128, 2048], F32)
        CBT = sb("CBT", [128, 104], F32)
        NEGB = sb("NEGB", [128, 16], F32)
        WRT = sb("WRT", [128, 64], BF16)
        RBT = sb("RBTt", [128, 8], F32)
        PERM = sb("PERMt", [128, 128], F32)
        IDENT = sb("IDENTt", [128, 128], F32)
        ONES1 = sb("ONES1t", [1, 128], F32)
        ONESB = sb("ONESBt", [128, 1], BF16)
        ONESW = sb("ONESWt", [128, 64], BF16)
        KS2B = sb("KS2Bt", [128, 8], F32)
        PSB = [stk.enter_context(nc.psum_tensor(f"PSB{j}", [128, NT], F32)) for j in range(8)]
        sems = {s: stk.enter_context(nc.semaphore(s)) for s in SEM_NAMES}
        block = stk.enter_context(nc.Block())
        P = Prog(nc, block, sems)

        S_PRT, S_CWT = 0, 128
        S_M0, S_M1, S_C0, S_F1, S_THR = 256, 272, 288, 304, 320
        S_KVS, S_PACKN, S_PACK = 336, 352, 448
        S_PACKKV = 464
        S_KSN, S_KS, S_JNK = 496, 528, 536
        ROWM = SMALL[0:1, 512:512 + NT]
        CWFB = SMALL[0:8, 512:512 + NT]
        ROWA = SMALL[0:1, 1024:1024 + NT]
        ROWV = SMALL[0:1, 1536:1536 + NT]

        def pb(seg):
            return POOLB[:, seg * T:(seg + 1) * T]

        def pbk(seg, q):
            return ("PBR", seg, q)

        def xk(c, n):
            return ("X", c, n)

        def xbk(c, n):
            return ("XBF", c, n)

        def qfk(c, n):
            return ("QF", c, n)

        def sck(j, n):
            return ("SCR", j, n)

        def dma_sp(dst, src, reads, writes):
            P.emit_dma("sp", lambda s, a=dst, b=src: s.dma_start(out=a, in_=b),
                       reads=reads, writes=writes)

        def dma_gp(dst, src, reads, writes):
            P.emit_dma("gp", lambda g, a=dst, b=src: g.dma_start(out=a, in_=b),
                       reads=reads, writes=writes)

        def tap(name, ap, keys, dt=F32):
            if name not in TAP_ENABLE:
                return
            h = nc.dram_tensor(f"tap_{name}", list(ap.shape), dt, kind="ExternalOutput").ap()
            dma_sp(h, ap, tuple(keys), ((f"tap_{name}",),))

        class WRing:
            def __init__(self):
                self.n = 0

            def load(self, dram_ap):
                s = self.n % 3
                self.n += 1
                key = ("WR", s)
                P.emit_dma("gp", lambda g, d=dram_ap, t=RING[s]: g.dma_start(out=t[:], in_=d),
                           reads=(), writes=(key,))
                return RING[s], key

        ring = WRing()

        def mm_chain(ps_ap, pieces, reads, wkey):
            n = len(pieces)
            fns = [(lambda t, lh=lh, rh=rh, st=(j == 0), sp=(j == n - 1), o=ps_ap:
                    t.matmul(o, lh, rh, start=st, stop=sp))
                   for j, (lh, rh) in enumerate(pieces)]
            P.emit("pe", fns, reads=reads, writes=(wkey,))

        for c in range(DC):
            dma_sp(X[c][:], xin[c * 128:(c + 1) * 128, :], (),
                   tuple(xk(c, n) for n in range(NN)))
        dma_sp(PERM[:], permd, (), (("PERM",),))
        dma_sp(IDENT[:], identd, (), (("IDENT",),))
        dma_sp(ONES1[:], ones1d, (), (("ONES1",),))
        dma_sp(ONESB[:], onesbd, (), (("ONESB",),))
        dma_sp(ONESW[:], oneswd, (), (("ONESW",),))
        dma_sp(RBT[:], rbd, (), (("RBT",),))
        dma_gp(WRT[:], wrd, (), (("WRT",),))

        def emit_layernorm(eps, sc_col, bi_col, bias_key, out_fp32, ns=(0, 1, 2, 3)):
            def rowm(p):
                return SCR[1][0:1, p * 512:(p + 1) * 512]

            def rowa(p):
                return SCR[1][0:1, 1024 + p * 512: 1536 + p * 512]

            def rowv(p):
                return SMALL[0:1, 512 + p * 512: 1024 + p * 512]
            KM = lambda p: ("ROWM", p)
            KA = lambda p: ("ROWA", p)
            KV_ = lambda p: ("ROWV", p)
            lo, hi = min(ns) * NT, (max(ns) + 1) * NT
            for c in range(DC):
                P.emit("dve", lambda e, c=c, lo=lo, hi=hi: e.tensor_copy(
                    XBF[c][:, lo:hi], X[c][:, lo:hi]),
                    reads=[xk(c, n) for n in ns],
                    writes=[xbk(c, n) for n in ns])
                P.emit("act", lambda e, c=c, lo=lo, hi=hi: e.activation(
                    QF[c][:, lo:hi], XBF[c][:, lo:hi], AF.Square),
                    reads=[xbk(c, n) for n in ns],
                    writes=[qfk(c, n) for n in ns])

            def emit_stats(n):
                bs = 4 + 2 * (n % 2)
                p = n % 2
                mm_chain(PSB[bs][0:1, :],
                         [(ONESB[:], XBF[c][:, nsl(n)]) for c in range(DC)],
                         reads=[("ONESB",)] + [xbk(c, n) for c in range(DC)],
                         wkey=("PSB", bs))
                mm_chain(PSB[bs + 1][0:1, :],
                         [(ONESB[:], QF[c][:, nsl(n)]) for c in range(DC)],
                         reads=[("ONESB",)] + [qfk(c, n) for c in range(DC)],
                         wkey=("PSB", bs + 1))
                amk = [KM(p), sck(1, p)]
                aak = [KA(p), sck(1, 2 + p)]
                avk = [KV_(p)] + ([("CWFB",)] if p == 0 else [])
                P.emit("act", lambda e, bs=bs, p=p: e.activation(rowm(p), PSB[bs][0:1, :], AF.Identity, scale=np.float32(1.0 / D)),
                       reads=[("PSB", bs)], writes=amk)
                P.emit("act", lambda e, bs=bs, p=p: e.activation(rowv(p), PSB[bs + 1][0:1, :], AF.Identity, scale=np.float32(1.0 / D)),
                       reads=[("PSB", bs + 1)], writes=avk)
                P.emit("dve", lambda e, p=p: e.tensor_tensor(rowa(p), rowm(p), rowm(p), ALU.mult),
                       reads=amk, writes=aak)
                P.emit("dve", lambda e, p=p: e.tensor_tensor(rowv(p), rowv(p), rowa(p), ALU.subtract),
                       reads=avk + aak, writes=avk)
                P.emit("act", lambda e, eps=eps, p=p: e.activation(rowv(p), rowv(p), AF.Ln, bias=np.float32(eps)),
                       reads=avk, writes=avk)
                P.emit("act", lambda e, p=p: e.activation(rowa(p), rowv(p), AF.Exp, scale=np.float32(-0.5)),
                       reads=avk, writes=aak)
                P.emit("dve", lambda e, p=p: e.tensor_tensor(rowm(p), rowm(p), rowa(p), ALU.mult),
                       reads=amk + aak, writes=amk)

            def emit_bcast(n):
                bs = 4 + 2 * (n % 2)
                p = n % 2
                P.emit("pe", lambda t, bs=bs, p=p: t.matmul(PSB[bs][:, :], ONES1[:], rowa(p), start=True, stop=True),
                       reads=[("ONES1",), KA(p), sck(1, 2 + p)], writes=[("PSB", bs)])
                P.emit("pe", lambda t, bs=bs, p=p: t.matmul(PSB[bs + 1][:, :], ONES1[:], rowm(p), start=True, stop=True),
                       reads=[("ONES1",), KM(p), sck(1, p)], writes=[("PSB", bs + 1)])

            def emit_apply(n):
                bs = 4 + 2 * (n % 2)
                for c in range(DC):
                    P.emit("dve", lambda e, c=c, n=n, bs=bs: e.tensor_tensor(
                        SCR[0][:, nsl(n)], X[c][:, nsl(n)], PSB[bs][:, :], ALU.mult),
                        reads=[xk(c, n), ("PSB", bs)], writes=[sck(0, n)])
                    if sc_col is None:
                        if out_fp32:
                            P.emit("dve", lambda e, c=c, n=n, bs=bs: e.tensor_tensor(
                                X[c][:, nsl(n)], SCR[0][:, nsl(n)], PSB[bs + 1][:, :], ALU.subtract),
                                reads=[sck(0, n), ("PSB", bs + 1)], writes=[xk(c, n)])
                            P.emit("act", lambda e, c=c, n=n: e.activation(
                                XBF[c][:, nsl(n)], X[c][:, nsl(n)], AF.Identity),
                                reads=[xk(c, n)], writes=[xbk(c, n)])
                        else:
                            P.emit("dve", lambda e, c=c, n=n, bs=bs: e.tensor_tensor(
                                XBF[c][:, nsl(n)], SCR[0][:, nsl(n)], PSB[bs + 1][:, :], ALU.subtract),
                                reads=[sck(0, n), ("PSB", bs + 1)], writes=[xbk(c, n)])
                    else:
                        P.emit("dve", lambda e, c=c, n=n, bs=bs: e.tensor_tensor(
                            SCR[2][:, nsl(n)], SCR[0][:, nsl(n)], PSB[bs + 1][:, :], ALU.subtract),
                            reads=[sck(0, n), ("PSB", bs + 1)], writes=[sck(2, n)])
                        P.emit("act", lambda e, c=c, n=n: e.activation(
                            X[c][:, nsl(n)], SCR[2][:, nsl(n)], AF.Identity,
                            bias=bi_col(c), scale=sc_col(c)),
                            reads=[sck(2, n), bias_key], writes=[xk(c, n)])
                        P.emit("dve", lambda e, c=c, n=n: e.tensor_copy(
                            XBF[c][:, nsl(n)], X[c][:, nsl(n)]),
                            reads=[xk(c, n)], writes=[xbk(c, n)])

            sched = list(ns)
            for j, n in enumerate(sched):
                emit_stats(n)
                if j >= 1:
                    emit_bcast(sched[j - 1])
                    emit_apply(sched[j - 1])
            emit_bcast(sched[-1])
            emit_apply(sched[-1])

        def emit_attn_a_layer(i):
            dma_sp(CBT[:], cbd[i], (("CBc",),), (("CBc",),))

            def bcol(j):
                return CBT[:, j:j + 1]

            P.emit("dve", lambda e: e.tensor_scalar(NEGB[:], CBT[:, 0:16], -1.0, 0.0, ALU.mult),
                   reads=[("CBc",)], writes=[("NEGB",)])

            def nbcol(j):
                return NEGB[:, j:j + 1]

            def proj(wt, wkey, off, n, bank):
                mm_chain(PSB[bank][:, :],
                         [(wt[:, off + k * 128: off + (k + 1) * 128], XBF[k][:, nsl(n)]) for k in range(DC)],
                         reads=[wkey] + [xbk(k, n) for k in range(DC)],
                         wkey=("PSB", bank))

            def fmap(c, n, bank, bias_j, nbias_j, dst_ap, dst_keys, acc=False):
                accr = SMALL[:, S_PACKN + c * 8 + 4 + n: S_PACKN + c * 8 + 5 + n] if acc else None
                acce = SMALL[:, S_PACKN + c * 8 + n: S_PACKN + c * 8 + 1 + n] if acc else None
                P.emit("act", lambda e, n=n, accr=accr: e.activation(
                    SCR[0][:, nsl(n)], PSB[bank][:, :], AF.Relu, bias=bcol(bias_j),
                    scale=np.float32(1.0), accum_out=accr),
                    reads=[("PSB", bank), ("CBc",)],
                    writes=[sck(0, n)] + ([("PKN", 1, c, n)] if acc else []))
                P.emit("act", lambda e, n=n: e.activation(
                    SCR[1][:, nsl(n)], PSB[bank][:, :], AF.Relu, bias=nbcol(nbias_j), scale=np.float32(-1.0)),
                    reads=[("PSB", bank), ("NEGB",)], writes=[sck(1, n)])
                P.emit("act", lambda e, n=n, acce=acce: e.activation(
                    SCR[2][:, nsl(n)], SCR[1][:, nsl(n)], AF.Exp, scale=np.float32(-1.0),
                    accum_out=acce),
                    reads=[sck(1, n)],
                    writes=[sck(2, n)] + ([("PKN", 2, c, n)] if acc else []))
                P.emit("dve", lambda e, n=n, dst_ap=dst_ap: e.tensor_tensor(
                    dst_ap, SCR[0][:, nsl(n)], SCR[2][:, nsl(n)], ALU.add),
                    reads=[sck(0, n), sck(2, n)], writes=dst_keys)

            for c in range(DC):
                wt, wkey = ring.load(aw[i, c])
                for n in range(NN):
                    pg = n % 4
                    pv = 4 + n % 2
                    proj(wt, wkey, 0, n, pg)
                    proj(wt, wkey, 1024, n, pv)
                    fmap(c, n, pg, 8 + c, 8 + c, SCR[0][:, nsl(n)], [sck(0, n)], acc=True)
                    P.emit("act", lambda e, c=c, n=n, pv=pv: e.activation(
                        SCR[1][:, nsl(n)], PSB[pv][:, :], AF.Identity, bias=bcol(16 + c), scale=np.float32(1.0)),
                        reads=[("PSB", pv), ("CBc",)], writes=[sck(1, n)])
                    P.emit("dve", lambda e, n=n: e.tensor_tensor(
                        SCR[1][:, nsl(n)], SCR[0][:, nsl(n)], SCR[1][:, nsl(n)], ALU.mult),
                        reads=[sck(0, n), sck(1, n)], writes=[sck(1, n)])
                    P.emit("dve", lambda e, c=c, n=n: e.tensor_scalar(
                        SCR[2][:, nsl(n)], SCR[1][:, nsl(n)], 1.0, 0.0, ALU.mult, ALU.add,
                        accum_out=SMALL[:, S_PACKKV + c * 4 + n: S_PACKKV + c * 4 + n + 1]),
                        reads=[sck(1, n)], writes=[sck(2, n), ("PKN", 0, c, n)])
            for c in range(DC):
                P.emit("dve", lambda e, c=c: e.tensor_reduce(
                    SMALL[:, S_PACK + c: S_PACK + c + 1],
                    SMALL[:, S_PACKKV + c * 4: S_PACKKV + (c + 1) * 4], AXX, ALU.add),
                    reads=[("PKN", 0, c, n) for n in range(NN)], writes=[("PACK",)])
                P.emit("dve", lambda e, c=c: e.tensor_reduce(
                    SMALL[:, S_PACK + 8 + c: S_PACK + 9 + c],
                    SMALL[:, S_PACKN + c * 8: S_PACKN + (c + 1) * 8], AXX, ALU.add),
                    reads=[("PKN", 1, c, n) for n in range(NN)] + [("PKN", 2, c, n) for n in range(NN)],
                    writes=[("PACK",)])
            if i == 0:
                tap("pack", SMALL[:, S_PACK:S_PACK + 16], [("PACK",)])
            pre = [ring.load(aw[i, 8 + j]) for j in range(3)]
            dma_sp(cc1i.ap(), SMALL[:, S_PACK:S_PACK + 16], (("PACK",),), (("CC1I",),))
            P.emit("gp", lambda g: g.collective_compute(
                "AllReduce", ALU.add, replica_groups=[[0, 1], [2, 3], [4, 5], [6, 7]],
                ins=[cc1i.ap().opt()], outs=[cc1o.ap().opt()]),
                reads=[("CC1I",)], writes=[("CC1O",)], sem="s_gcc", inc=1)
            dma_sp(SMALL[:, S_KVS:S_KVS + 16], cc1o.ap(), (("CC1O",),), (("KVS",),))

            if i == 0:
                tap("kvs", SMALL[:, S_KVS:S_KVS + 16], [("KVS",)])
            def emit_q(H):
                for j in range(4):
                    if H == 0 and j < 3:
                        wt, wkey = pre[j]
                    else:
                        wt, wkey = ring.load(aw[i, 8 + j])
                    for ch in range(2):
                        c = 2 * j + ch
                        for n in (2 * H, 2 * H + 1):
                            pg = 2 * ch + n % 2
                            proj(wt, wkey, ch * 1024, n, pg)
                            fmap(c, n, pg, c, c, QF[c][:, nsl(n)], [qfk(c, n)])

            def emit_att(H):
                for n in (2 * H, 2 * H + 1):
                    for c in range(DC):
                        P.emit("act", lambda e, c=c, n=n: e.activation(
                            SCR[0][:, nsl(n)], QF[c][:, nsl(n)], AF.Identity,
                            scale=SMALL[:, S_KVS + 8 + c: S_KVS + 9 + c], bias=np.float32(1e-6)),
                            reads=[qfk(c, n), ("KVS",)], writes=[sck(0, n)])
                        P.emit("dve", lambda e, n=n: e.reciprocal(SCR[1][:, nsl(n)], SCR[0][:, nsl(n)]),
                               reads=[sck(0, n)], writes=[sck(1, n)])
                        P.emit("act", lambda e, c=c, n=n: e.activation(
                            SCR[0][:, nsl(n)], QF[c][:, nsl(n)], AF.Identity,
                            scale=SMALL[:, S_KVS + c: S_KVS + c + 1]),
                            reads=[qfk(c, n), ("KVS",)], writes=[sck(0, n)])
                        P.emit("dve", lambda e, c=c, n=n: e.tensor_tensor(
                            XBF[c][:, nsl(n)], SCR[0][:, nsl(n)], SCR[1][:, nsl(n)], ALU.mult),
                            reads=[sck(0, n), sck(1, n)], writes=[xbk(c, n)])

            def emit_wo(H):
                for j in range(4):
                    wt, wkey = ring.load(aw[i, 12 + j])
                    for n in (2 * H, 2 * H + 1):
                        for ch in range(2):
                            m = 2 * j + ch
                            pg = 2 * ch + n % 2
                            proj(wt, wkey, ch * 1024, n, pg)
                            P.emit("act", lambda e, m=m, n=n, pg=pg: e.activation(
                                SCR[2][:, nsl(n)], PSB[pg][:, :], AF.Gelu_apprx_tanh,
                                bias=bcol(24 + m), scale=np.float32(1.0)),
                                reads=[("PSB", pg), ("CBc",)], writes=[sck(2, n)])
                            P.emit("dve", lambda e, m=m, n=n: e.tensor_tensor(
                                X[m][:, nsl(n)], X[m][:, nsl(n)], SCR[2][:, nsl(n)], ALU.add),
                                reads=[xk(m, n), sck(2, n)], writes=[xk(m, n)])

            def hv(m):
                if m < 16:
                    return QF[m // 2][:, (m % 2) * 1024:(m % 2) * 1024 + 1024]
                j = m - 16
                return POOLB[:, (j // 2) * T + (j % 2) * 1024:(j // 2) * T + (j % 2) * 1024 + 1024]

            def hkeys(m, nn_):
                if m < 16:
                    return [qfk(m // 2, (m % 2) * 2 + nn_)]
                j = m - 16
                return [pbk(j // 2, (j % 2) * 2 + nn_)]

            def emit_ffn_half(h):
                for jm in range(16):
                    wt, wkey = ring.load(aw[i, 16 + jm])
                    for half in range(2):
                        m = 2 * jm + half
                        for nn_ in range(2):
                            n = 2 * h + nn_
                            pg = nn_
                            proj(wt, wkey, half * 1024, n, pg)
                            P.emit("act", lambda e, m=m, n=n, nn_=nn_, pg=pg: e.activation(
                                hv(m)[:, nsl(nn_)], PSB[pg][:, :], AF.Gelu_apprx_tanh,
                                bias=bcol(32 + m), scale=np.float32(1.0)),
                                reads=[("PSB", pg), ("CBc",)], writes=hkeys(m, nn_))
                for mo in range(DC):
                    wt0, wk0 = ring.load(aw[i, 32 + 2 * mo])
                    wt1, wk1 = ring.load(aw[i, 33 + 2 * mo])
                    for nn_ in range(2):
                        n = 2 * h + nn_
                        pg = nn_
                        pieces = [(wt0[:, k * 128:(k + 1) * 128], hv(k)[:, nsl(nn_)]) for k in range(16)]
                        pieces += [(wt1[:, (k - 16) * 128:(k - 15) * 128], hv(k)[:, nsl(nn_)]) for k in range(16, 32)]
                        rd = [wk0, wk1]
                        for k in range(32):
                            rd += hkeys(k, nn_)
                        mm_chain(PSB[pg][:, :], pieces, reads=rd, wkey=("PSB", pg))
                        P.emit("act", lambda e, mo=mo, n=n, pg=pg: e.activation(
                            SCR[2][:, nsl(n)], PSB[pg][:, :], AF.Gelu_apprx_tanh,
                            bias=bcol(64 + mo), scale=np.float32(1.0)),
                            reads=[("PSB", pg), ("CBc",)], writes=[sck(2, n)])
                        P.emit("dve", lambda e, mo=mo, n=n: e.tensor_tensor(
                            X[mo][:, nsl(n)], X[mo][:, nsl(n)], SCR[2][:, nsl(n)], ALU.add),
                            reads=[xk(mo, n), sck(2, n)], writes=[xk(mo, n)])

            emit_q(0)
            emit_q(1)
            emit_att(0)
            emit_att(1)
            emit_wo(0)
            emit_wo(1)
            emit_layernorm(EPS_IN, lambda c: bcol(72 + c), lambda c: bcol(80 + c),
                           ("CBc",), out_fp32=True, ns=(0, 1))
            emit_ffn_half(0)
            emit_layernorm(EPS_IN, lambda c: bcol(72 + c), lambda c: bcol(80 + c),
                           ("CBc",), out_fp32=True, ns=(2, 3))
            emit_layernorm(EPS_IN, lambda c: bcol(88 + c), lambda c: bcol(96 + c),
                           ("CBc",), out_fp32=True, ns=(0, 1))
            emit_ffn_half(1)
            emit_layernorm(EPS_IN, lambda c: bcol(88 + c), lambda c: bcol(96 + c),
                           ("CBc",), out_fp32=True, ns=(2, 3))

        def emit_ffn():
            for tt in range(16):
                pieces = [(XBF[k][:, tt * 128:(tt + 1) * 128], WRT[:, k * 8:(k + 1) * 8])
                          for k in range(DC)]
                mm_chain(PSB[0][:, 0:8], pieces,
                         reads=[("WRT",)] + [xbk(k, tt // 4) for k in range(DC)],
                         wkey=("PSB", 0))
                P.emit("dve", lambda e: e.tensor_tensor(
                    SMALL[:, S_JNK:S_JNK + 8], PSB[0][:, 0:8], RBT[:], ALU.add),
                    reads=[("PSB", 0), ("RBT",)], writes=[("JNK",)])
                P.emit("act", lambda e, tt=tt: e.activation(
                    SMALL[:, S_PRT + tt * 8:S_PRT + (tt + 1) * 8],
                    SMALL[:, S_JNK:S_JNK + 8], AF.Sigmoid),
                    reads=[("JNK",)], writes=[("PRT", tt)])
            for tt in range(16):
                prt = SMALL[:, S_PRT + tt * 8:S_PRT + (tt + 1) * 8]
                m0 = SMALL[:, S_M0 + tt:S_M0 + tt + 1]
                m1 = SMALL[:, S_M1 + tt:S_M1 + tt + 1]
                c0 = SMALL[:, S_C0 + tt:S_C0 + tt + 1]
                f1 = SMALL[:, S_F1 + tt:S_F1 + tt + 1]
                thr = SMALL[:, S_THR + tt:S_THR + tt + 1]
                jnk = SMALL[:, S_JNK:S_JNK + 8]
                cwt = SMALL[:, S_CWT + tt * 8:S_CWT + (tt + 1) * 8]
                P.emit("dve", lambda e, prt=prt, m0=m0: e.tensor_reduce(m0, prt, AXX, ALU.max),
                       reads=[("PRT", tt)], writes=[("M0", tt)])
                P.emit("dve", lambda e, prt=prt, m0=m0, jnk=jnk, c0=c0: e.tensor_scalar(
                    jnk, prt, m0, 0.0, ALU.is_ge, ALU.add, accum_out=c0),
                    reads=[("PRT", tt), ("M0", tt)], writes=[("JNK",), ("C0", tt)])
                P.emit("dve", lambda e, jnk=jnk: e.tensor_scalar(jnk, jnk, -1e30, 0.0, ALU.mult),
                       reads=[("JNK",)], writes=[("JNK",)])
                P.emit("dve", lambda e, prt=prt, jnk=jnk: e.tensor_tensor(jnk, jnk, prt, ALU.add),
                       reads=[("JNK",), ("PRT", tt)], writes=[("JNK",)])
                P.emit("dve", lambda e, jnk=jnk, m1=m1: e.tensor_reduce(m1, jnk, AXX, ALU.max),
                       reads=[("JNK",)], writes=[("M1", tt)])
                P.emit("dve", lambda e, c0=c0, f1=f1: e.tensor_scalar(f1, c0, 1.0, 0.0, ALU.is_le),
                       reads=[("C0", tt)], writes=[("F1", tt)])
                P.emit("dve", lambda e, m1=m1, m0=m0, thr=thr: e.tensor_tensor(thr, m1, m0, ALU.subtract),
                       reads=[("M1", tt), ("M0", tt)], writes=[("THR", tt)])
                P.emit("dve", lambda e, thr=thr, f1=f1: e.tensor_tensor(thr, thr, f1, ALU.mult),
                       reads=[("THR", tt), ("F1", tt)], writes=[("THR", tt)])
                P.emit("dve", lambda e, thr=thr, m0=m0: e.tensor_tensor(thr, thr, m0, ALU.add),
                       reads=[("THR", tt), ("M0", tt)], writes=[("THR", tt)])
                P.emit("dve", lambda e, prt=prt, thr=thr, cwt=cwt: e.tensor_scalar(
                    cwt, prt, thr, 0.0, ALU.is_ge),
                    reads=[("PRT", tt), ("THR", tt)], writes=[("CWT", tt)])
                P.emit("dve", lambda e, cwt=cwt, prt=prt: e.tensor_tensor(cwt, cwt, prt, ALU.mult),
                       reads=[("CWT", tt), ("PRT", tt)], writes=[("CWT", tt)])

            for n in range(NN):
                for s4 in range(4):
                    tt = n * 4 + s4
                    P.emit("pe", lambda t, tt=tt, s4=s4: t.transpose(
                        PSB[7][0:8, s4 * 128:(s4 + 1) * 128],
                        SMALL[:, S_CWT + tt * 8:S_CWT + (tt + 1) * 8], IDENT[:]),
                        reads=[("CWT", tt), ("IDENT",)],
                        writes=[("PSB", 7)] if s4 == 3 else [("PSB7p", s4)])
                P.emit("dve", lambda e: e.tensor_copy(CWFB, PSB[7][0:8, :]),
                       reads=[("PSB", 7)] + [("PSB7p", s4) for s4 in range(3)],
                       writes=[("CWFB",), ("ROWM",)])
                dma_sp(cwrows.ap()[:, nsl(n)], CWFB, (("CWFB",),), (("CWROWS", n),))
            for ex in range(9):
                def hv2(m):
                    return QF[m] if m < 8 else pb(m - 8)

                def hk2(m, n):
                    return qfk(m, n) if m < 8 else pbk(m - 8, n)
                for m in range(16):
                    wt, wkey = ring.load(fw[ex, m])
                    for n in range(NN):
                        pg = n % 2
                        pu = 4 + n % 2
                        mm_chain(PSB[pg][:, :],
                                 [(wt[:, k * 128:(k + 1) * 128], XBF[k][:, nsl(n)]) for k in range(DC)],
                                 reads=[wkey] + [xbk(k, n) for k in range(DC)],
                                 wkey=("PSB", pg))
                        mm_chain(PSB[pu][:, :],
                                 [(wt[:, 1024 + k * 128:1024 + (k + 1) * 128], XBF[k][:, nsl(n)]) for k in range(DC)],
                                 reads=[wkey] + [xbk(k, n) for k in range(DC)],
                                 wkey=("PSB", pu))
                        P.emit("act", lambda e, n=n, pg=pg: e.activation(
                            SCR[2][:, nsl(n)], PSB[pg][:, :], AF.Silu),
                            reads=[("PSB", pg)], writes=[sck(2, n)])
                        P.emit("dve", lambda e, m=m, n=n, pu=pu: e.tensor_tensor(
                            hv2(m)[:, nsl(n)], SCR[2][:, nsl(n)], PSB[pu][:, :], ALU.mult),
                            reads=[sck(2, n), ("PSB", pu)], writes=[hk2(m, n)])
                for mo in range(DC):
                    wt, wkey = ring.load(fw[ex, 16 + mo])
                    for n in range(NN):
                        pg = n % 2
                        rd = [wkey] + [hk2(k, n) for k in range(16)]
                        mm_chain(PSB[pg][:, :],
                                 [(wt[:, k * 128:(k + 1) * 128], hv2(k)[:, nsl(n)]) for k in range(16)],
                                 reads=rd, wkey=("PSB", pg))
                        if ex == 0:
                            P.emit("dve", lambda e, mo=mo, n=n, pg=pg: e.tensor_tensor(
                                X[mo][:, nsl(n)], X[mo][:, nsl(n)], PSB[pg][:, :], ALU.add),
                                reads=[xk(mo, n), ("PSB", pg)], writes=[xk(mo, n)])
                        else:
                            if mo == 0:
                                srow = cwrows.ap()[ex - 1:ex, nsl(n)].squeeze(0).partition_broadcast(128)
                                dma_sp(SCR[2][:, nsl(n)], srow, (("CWROWS", n),), (sck(2, n),))
                            P.emit("dve", lambda e, n=n, pg=pg: e.tensor_tensor(
                                SCR[1][:, nsl(n)], PSB[pg][:, :], SCR[2][:, nsl(n)], ALU.mult),
                                reads=[("PSB", pg), sck(2, n)], writes=[sck(1, n)])
                            P.emit("dve", lambda e, mo=mo, n=n: e.tensor_tensor(
                                X[mo][:, nsl(n)], X[mo][:, nsl(n)], SCR[1][:, nsl(n)], ALU.add),
                                reads=[xk(mo, n), sck(1, n)], writes=[xk(mo, n)])

        def emit_attn_b():
            dma_gp(pb(0), cosd, (), tuple(pbk(0, q) for q in range(NN)))
            dma_gp(pb(1), sind, (), tuple(pbk(1, q) for q in range(NN)))
            def kftv(s):
                return POOLB[:, (2 + s // 2) * T + (s % 2) * 1024: (2 + s // 2) * T + (s % 2) * 1024 + 1024]

            def kftkeys(s):
                return [pbk(2 + s // 2, (s % 2) * 2), pbk(2 + s // 2, (s % 2) * 2 + 1)]

            def vtv(s):
                return POOLB[:, (4 + s // 2) * T + (s % 2) * 1024: (4 + s // 2) * T + (s % 2) * 1024 + 1024]

            def vtkeys(s):
                return [pbk(4 + s // 2, (s % 2) * 2), pbk(4 + s // 2, (s % 2) * 2 + 1)]

            for g in range(NN):
                for j in range(12):
                    wt, wkey = ring.load(bw[j])
                    for half in range(2):
                        cc = 2 * j + half
                        pg = cc % 2
                        mm_chain(PSB[pg][:, :],
                                 [(wt[:, half * 1024 + k * 128: half * 1024 + (k + 1) * 128],
                                   XBF[k][:, nsl(g)]) for k in range(DC)],
                                 reads=[wkey] + [xbk(k, g) for k in range(DC)],
                                 wkey=("PSB", pg))
                        if cc < 16:
                            c = cc % 8
                            isq = cc < 8
                            P.emit("act", lambda e, pg=pg, g=g: e.activation(
                                SCR[0][:, nsl(g)], PSB[pg][:, :], AF.Identity),
                                reads=[("PSB", pg)], writes=[sck(0, g)])
                            P.emit("pe", lambda t, g=g: t.matmul(
                                PSB[2][:, :], PERM[:], SCR[0][:, nsl(g)], start=True, stop=True),
                                reads=[("PERM",), sck(0, g)], writes=[("PSB", 2)])
                            P.emit("dve", lambda e, g=g: e.tensor_tensor(
                                SCR[1][:, nsl(g)], SCR[0][:, nsl(g)], pb(0)[:, nsl(g)], ALU.mult),
                                reads=[sck(0, g), pbk(0, g)], writes=[sck(1, g)])
                            P.emit("dve", lambda e, g=g: e.tensor_tensor(
                                SCR[0][:, nsl(g)], PSB[2][:, :], pb(1)[:, nsl(g)], ALU.mult),
                                reads=[("PSB", 2), pbk(1, g)], writes=[sck(0, g)])
                            P.emit("dve", lambda e, g=g: e.tensor_tensor(
                                SCR[0][:, nsl(g)], SCR[0][:, nsl(g)], SCR[1][:, nsl(g)], ALU.add),
                                reads=[sck(0, g), sck(1, g)], writes=[sck(0, g)])
                            sc = SCALE_B if isq else 1.0
                            P.emit("act", lambda e, g=g, sc=sc: e.activation(
                                SCR[1][:, nsl(g)], SCR[0][:, nsl(g)], AF.Relu, scale=np.float32(sc)),
                                reads=[sck(0, g)], writes=[sck(1, g)])
                            P.emit("act", lambda e, g=g, sc=sc: e.activation(
                                SCR[2][:, nsl(g)], SCR[0][:, nsl(g)], AF.Relu, scale=np.float32(-sc)),
                                reads=[sck(0, g)], writes=[sck(2, g)])
                            P.emit("act", lambda e, g=g: e.activation(
                                SCR[2][:, nsl(g)], SCR[2][:, nsl(g)], AF.Exp, scale=np.float32(-1.0)),
                                reads=[sck(2, g)], writes=[sck(2, g)])
                            if isq:
                                P.emit("dve", lambda e, c=c, g=g: e.tensor_tensor(
                                    QF[c][:, nsl(g)], SCR[2][:, nsl(g)], SCR[1][:, nsl(g)], ALU.add),
                                    reads=[sck(2, g), sck(1, g)], writes=[qfk(c, g)])
                            else:
                                P.emit("dve", lambda e, g=g: e.tensor_tensor(
                                    SCR[0][:, nsl(g)], SCR[2][:, nsl(g)], SCR[1][:, nsl(g)], ALU.add),
                                    reads=[sck(2, g), sck(1, g)], writes=[sck(0, g)])
                                P.emit("act", lambda e, c=c, g=g: e.activation(
                                    SCR[1][:, nsl(g)], SCR[0][:, nsl(g)], AF.Identity,
                                    accum_out=SMALL[:, S_KSN + c * 4 + g: S_KSN + c * 4 + g + 1]),
                                    reads=[sck(0, g)], writes=[sck(1, g), ("KSN", c, g)])
                                for s in range(4):
                                    P.emit("pe", lambda t, g=g, s=s: t.transpose(
                                        PSB[3][:, 0:128],
                                        SCR[0][:, g * NT + s * 128: g * NT + (s + 1) * 128], IDENT[:]),
                                        reads=[sck(0, g), ("IDENT",)], writes=[("PSB", 3)])
                                    P.emit("dve", lambda e, c=c, s=s: e.tensor_copy(
                                        kftv(s)[:, c * 128:(c + 1) * 128], PSB[3][:, 0:128]),
                                        reads=[("PSB", 3)], writes=kftkeys(s))
                        else:
                            c = cc - 16
                            P.emit("act", lambda e, pg=pg, g=g: e.activation(
                                SCR[0][:, nsl(g)], PSB[pg][:, :], AF.Identity),
                                reads=[("PSB", pg)], writes=[sck(0, g)])
                            for s in range(4):
                                P.emit("pe", lambda t, g=g, s=s: t.transpose(
                                    PSB[3][:, 0:128],
                                    SCR[0][:, g * NT + s * 128: g * NT + (s + 1) * 128], IDENT[:]),
                                    reads=[sck(0, g), ("IDENT",)], writes=[("PSB", 3)])
                                P.emit("dve", lambda e, c=c, s=s: e.tensor_copy(
                                    vtv(s)[:, c * 128:(c + 1) * 128], PSB[3][:, 0:128]),
                                    reads=[("PSB", 3)], writes=vtkeys(s))
                for lohi in range(2):
                    fns = []
                    tgt = PSB[4 + lohi]
                    for h in range(8):
                        hh = lohi * 8 + h
                        for s in range(4):
                            kft = kftv(s)[:, hh * 64:(hh + 1) * 64]
                            vt = vtv(s)[:, hh * 64:(hh + 1) * 64]
                            st = (g == 0 and h == 0 and s == 0)
                            sp = (g == NN - 1 and h == 7 and s == 3)
                            fns.append(lambda t, kft=kft, vt=vt, tgt=tgt, col=h * 64, st=st, sp=sp:
                                       t.matmul(tgt[0:64, col:col + 64], kft, vt, start=st, stop=sp))
                    rd = []
                    for s in range(4):
                        rd += kftkeys(s) + vtkeys(s)
                    P.emit("pe", fns, reads=rd, writes=[("PSB", 4 + lohi)])
            for c in range(DC):
                P.emit("dve", lambda e, c=c: e.tensor_reduce(
                    SMALL[:, S_KS + c: S_KS + c + 1],
                    SMALL[:, S_KSN + c * 4: S_KSN + (c + 1) * 4], AXX, ALU.add),
                    reads=[("KSN", c, g) for g in range(NN)], writes=[("KS",)])
            P.emit("dve", lambda e: e.tensor_copy(SCR[0][0:64, 0:NT], PSB[4][0:64, :]),
                   reads=[("PSB", 4)], writes=[sck(0, 0)])
            P.emit("dve", lambda e: e.tensor_copy(SCR[0][0:64, NT:2 * NT], PSB[5][0:64, :]),
                   reads=[("PSB", 5)], writes=[sck(0, 1)])
            dma_sp(cc2i.ap()[:, 0:1024], SCR[0][0:64, 0:1024], (sck(0, 0), sck(0, 1)), (("CC2I", 0),))
            dma_sp(cc2i.ap()[:, 1024:1032], SMALL[0:64, S_KS:S_KS + 8], (("KS",),), (("CC2I", 1),))
            dma_sp(cc2i.ap()[:, 1032:1040], SMALL[64:128, S_KS:S_KS + 8], (("KS",),), (("CC2I", 2),))
            P.emit("gp", lambda g: g.collective_compute(
                "AllReduce", ALU.add, replica_groups=[[0, 1], [2, 3], [4, 5], [6, 7]],
                ins=[cc2i.ap().opt()], outs=[cc2o.ap().opt()]),
                reads=[("CC2I", 0), ("CC2I", 1), ("CC2I", 2)], writes=[("CC2O",)], sem="s_gcc", inc=1)
            dma_sp(KS2B[0:64, :], cc2o.ap()[:, 1024:1032], (("CC2O",),), (("KS2B",),))
            dma_sp(KS2B[64:128, :], cc2o.ap()[:, 1032:1040], (("CC2O",),), (("KS2B",),))
            for c in range(DC):
                kvl = pb(2)[:, c * 128:(c + 1) * 128]
                ksl = pb(3)[:, c * 128:(c + 1) * 128]
                P.emit("gp", lambda g, kvl=kvl: g.memset(kvl, 0.0),
                       reads=(), writes=[("KVL", c), pbk(2, (c * 128) // NT)],
                       sem="s_gpo", inc=1)
                dma_gp(kvl[0:64, 0:64], cc2o.ap()[0:64, (2 * c) * 64:(2 * c + 1) * 64],
                       (("CC2O",), ("KVL", c)), (("KVL", c),))
                dma_gp(kvl[64:128, 64:128], cc2o.ap()[0:64, (2 * c + 1) * 64:(2 * c + 2) * 64],
                       (("CC2O",), ("KVL", c)), (("KVL", c),))
                P.emit("gp", lambda g, ksl=ksl: g.memset(ksl, 0.0),
                       reads=(), writes=[("KSL", c), pbk(3, (c * 128) // NT)],
                       sem="s_gpo", inc=1)
                P.emit("dve", lambda e, ksl=ksl, c=c: e.tensor_scalar(
                    ksl[0:64, 0:64], ONESW[0:64, :], KS2B[0:64, c:c + 1], 0.0, ALU.mult),
                    reads=[("ONESW",), ("KS2B",), ("KSL", c)], writes=[("KSL", c)])
                P.emit("dve", lambda e, ksl=ksl, c=c: e.tensor_scalar(
                    ksl[64:128, 64:128], ONESW[64:128, :], KS2B[64:128, c:c + 1], 0.0, ALU.mult),
                    reads=[("ONESW",), ("KS2B",), ("KSL", c)], writes=[("KSL", c)])
            for n in range(NN):
                for c in range(DC):
                    kvl = pb(2)[:, c * 128:(c + 1) * 128]
                    ksl = pb(3)[:, c * 128:(c + 1) * 128]
                    pg = n % 2
                    P.emit("pe", lambda t, c=c, n=n, kvl=kvl, pg=pg: t.matmul(
                        PSB[pg][:, :], kvl, QF[c][:, nsl(n)], start=True, stop=True),
                        reads=[("KVL", c), qfk(c, n)], writes=[("PSB", pg)])
                    P.emit("pe", lambda t, c=c, n=n, ksl=ksl: t.matmul(
                        PSB[2][:, :], ksl, QF[c][:, nsl(n)], start=True, stop=True),
                        reads=[("KSL", c), qfk(c, n)], writes=[("PSB", 2)])
                    P.emit("dve", lambda e, n=n: e.tensor_scalar(
                        SCR[0][:, nsl(n)], PSB[2][:, :], 1e-6, 0.0, ALU.max),
                        reads=[("PSB", 2)], writes=[sck(0, n)])
                    P.emit("dve", lambda e, n=n: e.reciprocal(SCR[1][:, nsl(n)], SCR[0][:, nsl(n)]),
                           reads=[sck(0, n)], writes=[sck(1, n)])
                    P.emit("dve", lambda e, c=c, n=n, pg=pg: e.tensor_tensor(
                        XBF[c][:, nsl(n)], PSB[pg][:, :], SCR[1][:, nsl(n)], ALU.mult),
                        reads=[("PSB", pg), sck(1, n)], writes=[xbk(c, n)])
            for j in range(4):
                wt, wkey = ring.load(bw[12 + j])
                for half in range(2):
                    mo = 2 * j + half
                    for n in range(NN):
                        pg = n % 2
                        mm_chain(PSB[pg][:, :],
                                 [(wt[:, half * 1024 + k * 128: half * 1024 + (k + 1) * 128],
                                   XBF[k][:, nsl(n)]) for k in range(DC)],
                                 reads=[wkey] + [xbk(k, n) for k in range(DC)],
                                 wkey=("PSB", pg))
                        P.emit("dve", lambda e, mo=mo, n=n, pg=pg: e.tensor_tensor(
                            X[mo][:, nsl(n)], X[mo][:, nsl(n)], PSB[pg][:, :], ALU.add),
                            reads=[xk(mo, n), ("PSB", pg)], writes=[xk(mo, n)])

        def phase_res1():
            for c in range(DC):
                j = c % 3
                P.emit_dma("sp", lambda s, c=c, j=j: s.dma_start(
                    out=SCR[j][:], in_=xin[c * 128:(c + 1) * 128, :]),
                    reads=[sck(j, n) for n in range(NN)],
                    writes=[sck(j, n) for n in range(NN)])
                P.emit("dve", lambda e, c=c, j=j: e.tensor_tensor(X[c][:], X[c][:], SCR[j][:], ALU.add),
                       reads=[xk(c, n) for n in range(NN)] + [sck(j, n) for n in range(NN)],
                       writes=[xk(c, n) for n in range(NN)])

        phases = [("oln1", lambda: emit_layernorm(EPS_OUT, None, None, None, out_fp32=True))]
        for i in range(NLAYER):
            phases.append((f"al{i}", lambda i=i: emit_attn_a_layer(i)))
        phases += [
            ("res1", phase_res1),
            ("oln2", lambda: (P.barrier(), emit_layernorm(EPS_OUT, None, None, None, out_fp32=False), P.barrier())),
            ("ffn1", emit_ffn),
            ("oln3", lambda: (P.barrier(), emit_layernorm(EPS_OUT, None, None, None, out_fp32=False), P.barrier())),
            ("attnb", emit_attn_b),
            ("oln4", lambda: (P.barrier(), emit_layernorm(EPS_OUT, None, None, None, out_fp32=False), P.barrier())),
            ("ffn2", emit_ffn),
        ]
        for name, fn in phases:
            fn()
            P.flush()
            if name == upto:
                break
        for c in range(DC):
            P.emit_dma("sp", lambda s, c=c: s.dma_start(out=outd[c * 128:(c + 1) * 128, :], in_=X[c][:]),
                       reads=[xk(c, n) for n in range(NN)], writes=[("OUT", c)])
        P.wait_only("sp", [("OUT", c) for c in range(DC)])
        P.flush()

    return nc



def _prep_inputs(inputs):
    inp = {k: np.asarray(v) for k, v in inputs.items()}
    aw, cb = _pack_attn_a(inp)
    fwp = _pack_ffn(inp)
    bwp = _pack_attn_b(inp)
    WrT = np.ascontiguousarray(inp["L0_ffn_Wr"].T)
    wr_t = np.ascontiguousarray(WrT.reshape(DC, 128, 8).transpose(1, 0, 2).reshape(128, 64))
    rbt = np.ascontiguousarray(np.broadcast_to(inp["L0_ffn_rb"].astype(np.float32), (128, 8)))
    x = inp["L0_attn_a_x"]
    in_maps = []
    for core in range(8):
        seq, h = core // 2, core % 2
        xs = np.ascontiguousarray(x[seq, h * T:(h + 1) * T, :].T)
        m = dict(x=xs, aw=aw, fw=fwp, bw=bwp, cb=cb, wr=wr_t, rbt=rbt)
        m.update(_host_consts(core))
        in_maps.append(m)
    return x, in_maps


def kernel(**inputs):
    x, in_maps = _prep_inputs(inputs)
    nc = build_program()
    res = run_bass_kernel_spmd(nc, in_maps, core_ids=list(range(8)))
    out = np.empty_like(x)
    for core in range(8):
        seq, h = core // 2, core % 2
        out[seq, h * T:(h + 1) * T, :] = res.results[core]["out"].T
    return out


# version 11
# speedup vs baseline: 1.0761x; 1.0761x over previous
import numpy as np
import ml_dtypes
import concourse.bass as bass
import concourse.mybir as mybir
from concourse.bass_utils import run_bass_kernel_spmd
from contextlib import ExitStack

F32 = mybir.dt.float32
BF16 = mybir.dt.bfloat16
ALU = mybir.AluOpType
AF = mybir.ActivationFunctionType
AXX = mybir.AxisListType.X

D = 1024
T = 2048
NT = 512
NN = T // NT
DC = D // 128
NLAYER = 8
EPS_OUT = 1e-5
EPS_IN = 1e-6
SCALE_B = 64.0 ** -0.5

ENGINES = ("pe", "act", "dve", "sp", "gp")
SECTION = {"pe": "tensor", "act": "scalar", "dve": "vector", "sp": "sync", "gp": "gpsimd"}
DEF_SEM = {"pe": "s_pe", "act": "s_act", "dve": "s_dve", "sp": None, "gp": None}
DEF_INC = {"pe": 1, "act": 1, "dve": 1, "sp": 16, "gp": 16}
SEM_NAMES = ("s_pe", "s_act", "s_dve",
             "s_sp0", "s_sp1", "s_sp2", "s_sp3",
             "s_gpd0", "s_gpd1", "s_gpd2", "s_gpd3",
             "s_gpo", "s_gcc")
TAP_ENABLE = set()


def nsl(n, w=NT):
    return slice(n * w, (n + 1) * w)



def _colblocks(W):
    din, dout = W.shape
    A = W.reshape(din // 128, 128, dout // 128, 128)
    return np.ascontiguousarray(A.transpose(2, 1, 0, 3).reshape(dout // 128, 128, din))


def _pack_attn_a(inp):
    aws, cbs = [], []
    for i in range(NLAYER):
        Wq = inp["L0_attn_a_Wq"][i].transpose(1, 0, 2).reshape(D, D)
        Wk = inp["L0_attn_a_Wk"][i].transpose(1, 0, 2).reshape(D, D)
        Wv = inp["L0_attn_a_Wv"][i].transpose(1, 0, 2).reshape(D, D)
        bq = _colblocks(np.ascontiguousarray(Wq))
        bk = _colblocks(np.ascontiguousarray(Wk))
        bv = _colblocks(np.ascontiguousarray(Wv))
        bo = _colblocks(inp["L0_attn_a_Wo"][i])
        b1 = _colblocks(inp["L0_attn_a_W1"][i])
        b2 = _colblocks(inp["L0_attn_a_W2"][i])
        units = []
        for c in range(DC):
            units.append(np.concatenate([bk[c], bv[c]], axis=1))
        for j in range(4):
            units.append(np.concatenate([bq[2 * j], bq[2 * j + 1]], axis=1))
        for j in range(4):
            units.append(np.concatenate([bo[2 * j], bo[2 * j + 1]], axis=1))
        for j in range(16):
            units.append(np.concatenate([b1[2 * j], b1[2 * j + 1]], axis=1))
        for m in range(DC):
            units.append(b2[m][:, 0:2048])
            units.append(b2[m][:, 2048:4096])
        aws.append(np.stack(units))

        def chunks(v):
            return np.ascontiguousarray(v.reshape(-1, 128).T)
        cb = np.concatenate([
            chunks(inp["L0_attn_a_bq"][i].reshape(-1)),
            chunks(inp["L0_attn_a_bk"][i].reshape(-1)),
            chunks(inp["L0_attn_a_bv"][i].reshape(-1)),
            chunks(inp["L0_attn_a_bo"][i]),
            chunks(inp["L0_attn_a_b1"][i]),
            chunks(inp["L0_attn_a_b2"][i]),
            chunks(inp["L0_attn_a_ln1_s"][i]),
            chunks(inp["L0_attn_a_ln1_b"][i]),
            chunks(inp["L0_attn_a_ln2_s"][i]),
            chunks(inp["L0_attn_a_ln2_b"][i]),
        ], axis=1)
        cbs.append(cb)
    return np.ascontiguousarray(np.stack(aws)), np.ascontiguousarray(np.stack(cbs))


def _pack_ffn(inp):
    out = []
    for e in range(9):
        if e == 0:
            Wg, Wu, Wd = inp["L0_ffn_Wg_s"], inp["L0_ffn_Wu_s"], inp["L0_ffn_Wd_s"]
        else:
            Wg, Wu, Wd = inp["L0_ffn_Wg"][e - 1], inp["L0_ffn_Wu"][e - 1], inp["L0_ffn_Wd"][e - 1]
        bg = _colblocks(np.ascontiguousarray(Wg.T))
        bu = _colblocks(np.ascontiguousarray(Wu.T))
        bd = _colblocks(np.ascontiguousarray(Wd.T))
        units = [np.concatenate([bg[m], bu[m]], axis=1) for m in range(16)]
        units += [bd[m] for m in range(DC)]
        out.append(np.stack(units))
    return np.ascontiguousarray(np.stack(out))


def _pack_attn_b(inp):
    bq = _colblocks(inp["L0_attn_b_W_qkv"])
    bo = _colblocks(inp["L0_attn_b_W_out"])
    units = [np.concatenate([bq[2 * j], bq[2 * j + 1]], axis=1) for j in range(12)]
    units += [np.concatenate([bo[2 * j], bo[2 * j + 1]], axis=1) for j in range(4)]
    return np.ascontiguousarray(np.stack(units))


def _host_consts(core):
    hd = 64
    t0 = (core % 2) * T
    inv_freq = 1.0 / (500000.0 ** (np.arange(0, hd, 2, dtype=np.float64) / hd))
    t = np.arange(t0, t0 + T, dtype=np.float64)
    emb = np.concatenate([t[:, None] * inv_freq[None, :]] * 2, axis=1)
    cos = np.cos(emb).astype(np.float32).T
    sin = np.sin(emb).astype(np.float32).T
    cos2 = np.ascontiguousarray(np.concatenate([cos, cos], axis=0))
    sin2 = np.ascontiguousarray(np.concatenate([sin, sin], axis=0))
    perm = np.zeros((128, 128), np.float32)
    for o in (0, 64):
        for j in range(32):
            perm[o + j + 32, o + j] = -1.0
            perm[o + j, o + j + 32] = 1.0
    ident = np.eye(128, dtype=np.float32)
    return dict(cost=cos2, sint=sin2, perm=perm, ident=ident,
                ones1=np.ones((1, 128), np.float32),
                onesb=np.ones((128, 1), ml_dtypes.bfloat16),
                onesw=np.ones((128, 64), ml_dtypes.bfloat16))



class Prog:
    def __init__(self, nc, block, sems):
        self.nc, self.block, self.sems = nc, block, sems
        self.q = {e: [] for e in ENGINES}
        self.cnt = {s: 0 for s in SEM_NAMES}
        self.waited = {e: {s: 0 for s in SEM_NAMES} for e in ENGINES}
        self.tiles = {}
        self.dma_n = {"sp": 0, "gp": 0}

    def _need(self, reads, writes):
        m = {}

        def add(tok):
            if tok is not None:
                s, v = tok
                if m.get(s, 0) < v:
                    m[s] = v
        for k in reads:
            t = self.tiles.get(k)
            if t:
                add(t[0])
        for k in writes:
            t = self.tiles.get(k)
            if t:
                add(t[0])
                for r in t[1]:
                    add(r)
        return m

    def emit_dma(self, eng, fn, reads=(), writes=()):
        lane = self.dma_n[eng] % 4
        self.dma_n[eng] += 1
        base = "s_sp" if eng == "sp" else "s_gpd"
        return self.emit(eng, fn, reads=tuple(reads) + (("dmalane", eng, lane),),
                         writes=tuple(writes) + (("dmalane", eng, lane),),
                         sem=f"{base}{lane}", inc=16)

    def emit(self, eng, fns, reads=(), writes=(), sem=None, inc=None):
        sem = sem or DEF_SEM[eng]
        assert sem is not None, "DMA queues must go through emit_dma"
        if inc is None:
            inc = DEF_INC[eng]
        need = self._need(reads, writes)
        waits = []
        for s, v in need.items():
            if self.waited[eng][s] < v:
                waits.append((s, v))
                self.waited[eng][s] = v
        if not isinstance(fns, (list, tuple)):
            fns = [fns]
        tok = None
        if inc and fns:
            self.cnt[sem] += inc
            tok = (sem, self.cnt[sem])
            for k in writes:
                self.tiles[k] = (tok, [])
            for k in reads:
                self.tiles.setdefault(k, (None, []))[1].append(tok)
        self.q[eng].append((waits, fns, sem, inc))
        return tok

    def wait_only(self, eng, keys):
        need = self._need(keys, ())
        waits = []
        for s, v in need.items():
            if self.waited[eng][s] < v:
                waits.append((s, v))
                self.waited[eng][s] = v
        if waits:
            self.q[eng].append((waits, [], None, 0))

    def barrier(self):
        toks = [(s, self.cnt[s]) for s in SEM_NAMES if self.cnt[s] > 0]
        for e in ENGINES:
            waits = []
            for s, v in toks:
                if self.waited[e][s] < v:
                    waits.append((s, v))
                    self.waited[e][s] = v
            if waits:
                self.q[e].append((waits, [], None, 0))

    def flush(self):
        for eng in ENGINES:
            steps = self.q[eng]
            if not steps:
                continue
            self.q[eng] = []
            sems = self.sems

            def section(e, steps=steps, sems=sems):
                state = {"sem": None, "inc": 0, "inst": None}

                def flushinc():
                    if state["inc"] and state["inst"] is not None:
                        state["inst"].then_inc(sems[state["sem"]], state["inc"])
                    state["sem"], state["inc"], state["inst"] = None, 0, None
                for waits, fns, sem, inc in steps:
                    if waits:
                        flushinc()
                        for s, v in waits:
                            e.wait_ge(sems[s], v)
                    last = None
                    for fn in fns:
                        last = fn(e)
                    if inc and last is not None:
                        if inc == 1 and sem in ("s_pe", "s_dve", "s_act", "s_gpo"):
                            if state["sem"] is not None and state["sem"] != sem:
                                flushinc()
                            state["sem"] = sem
                            state["inc"] += inc
                            state["inst"] = last
                        else:
                            flushinc()
                            last.then_inc(sems[sem], inc)
                flushinc()
            getattr(self.block, SECTION[eng])(section)



def build_program(upto="all"):
    nc = bass.Bass("TRN2", target_bir_lowering=False, debug=False)

    xin = nc.dram_tensor("x", [D, T], F32, kind="ExternalInput").ap()
    aw = nc.dram_tensor("aw", [NLAYER, 48, 128, 2048], F32, kind="ExternalInput").ap()
    fw = nc.dram_tensor("fw", [9, 24, 128, 2048], F32, kind="ExternalInput").ap()
    bw = nc.dram_tensor("bw", [16, 128, 2048], F32, kind="ExternalInput").ap()
    cbd = nc.dram_tensor("cb", [NLAYER, 128, 104], F32, kind="ExternalInput").ap()
    wrd = nc.dram_tensor("wr", [128, 64], F32, kind="ExternalInput").ap()
    rbd = nc.dram_tensor("rbt", [128, 8], F32, kind="ExternalInput").ap()
    cosd = nc.dram_tensor("cost", [128, T], F32, kind="ExternalInput").ap()
    sind = nc.dram_tensor("sint", [128, T], F32, kind="ExternalInput").ap()
    permd = nc.dram_tensor("perm", [128, 128], F32, kind="ExternalInput").ap()
    identd = nc.dram_tensor("ident", [128, 128], F32, kind="ExternalInput").ap()
    ones1d = nc.dram_tensor("ones1", [1, 128], F32, kind="ExternalInput").ap()
    onesbd = nc.dram_tensor("onesb", [128, 1], BF16, kind="ExternalInput").ap()
    oneswd = nc.dram_tensor("onesw", [128, 64], BF16, kind="ExternalInput").ap()
    outd = nc.dram_tensor("out", [D, T], F32, kind="ExternalOutput").ap()

    cc1i = nc.dram_tensor("cc1i", [128, 16], F32, kind="Internal")
    cc1o = nc.dram_tensor("cc1o", [128, 16], F32, kind="Internal")
    cwrows = nc.dram_tensor("cwrows", [8, T], F32, kind="Internal")
    cc2i = nc.dram_tensor("cc2i", [64, 1040], F32, kind="Internal")
    cc2o = nc.dram_tensor("cc2o", [64, 1040], F32, kind="Internal")

    with ExitStack() as stk:
        def sb(name, shape, dt):
            return stk.enter_context(nc.sbuf_tensor(name, shape, dt))

        X = [sb(f"X{c}", [128, T], F32) for c in range(DC)]
        XBF = [sb(f"XBF{c}", [128, T], BF16) for c in range(DC)]
        QF = [sb(f"QF{c}", [128, T], BF16) for c in range(DC)]
        SCR = [sb(f"SCR{j}", [128, T], F32) for j in range(3)]
        POOLB = sb("POOLB", [128, 8 * T], BF16)
        RING = [sb(f"RING{j}", [128, 2048], BF16) for j in range(3)]
        SMALL = sb("SMALL", [128, 2048], F32)
        CBT = sb("CBT", [128, 104], F32)
        NEGB = sb("NEGB", [128, 16], F32)
        WRT = sb("WRT", [128, 64], BF16)
        RBT = sb("RBTt", [128, 8], F32)
        PERM = sb("PERMt", [128, 128], F32)
        IDENT = sb("IDENTt", [128, 128], F32)
        ONES1 = sb("ONES1t", [1, 128], F32)
        ONESB = sb("ONESBt", [128, 1], BF16)
        ONESW = sb("ONESWt", [128, 64], BF16)
        KS2B = sb("KS2Bt", [128, 8], F32)
        PSB = [stk.enter_context(nc.psum_tensor(f"PSB{j}", [128, NT], F32)) for j in range(8)]
        sems = {s: stk.enter_context(nc.semaphore(s)) for s in SEM_NAMES}
        block = stk.enter_context(nc.Block())
        P = Prog(nc, block, sems)

        S_PRT, S_CWT = 0, 128
        S_M0, S_M1, S_C0, S_F1, S_THR = 256, 272, 288, 304, 320
        S_KVS, S_PACKN, S_PACK = 336, 352, 448
        S_PACKKV = 464
        S_KSN, S_KS, S_JNK = 496, 528, 536
        ROWM = SMALL[0:1, 512:512 + NT]
        CWFB = SMALL[0:8, 512:512 + NT]
        ROWA = SMALL[0:1, 1024:1024 + NT]
        ROWV = SMALL[0:1, 1536:1536 + NT]

        def pb(seg):
            return POOLB[:, seg * T:(seg + 1) * T]

        def pbk(seg, q):
            return ("PBR", seg, q)

        def xk(c, n):
            return ("X", c, n)

        def xbk(c, n):
            return ("XBF", c, n)

        def qfk(c, n):
            return ("QF", c, n)

        def sck(j, n):
            return ("SCR", j, n)

        def dma_sp(dst, src, reads, writes):
            P.emit_dma("sp", lambda s, a=dst, b=src: s.dma_start(out=a, in_=b),
                       reads=reads, writes=writes)

        def dma_gp(dst, src, reads, writes):
            P.emit_dma("gp", lambda g, a=dst, b=src: g.dma_start(out=a, in_=b),
                       reads=reads, writes=writes)

        def tap(name, ap, keys, dt=F32):
            if name not in TAP_ENABLE:
                return
            h = nc.dram_tensor(f"tap_{name}", list(ap.shape), dt, kind="ExternalOutput").ap()
            dma_sp(h, ap, tuple(keys), ((f"tap_{name}",),))

        class WRing:
            def __init__(self):
                self.n = 0

            def load(self, dram_ap):
                s = self.n % 3
                self.n += 1
                key = ("WR", s)
                P.emit_dma("gp", lambda g, d=dram_ap, t=RING[s]: g.dma_start(out=t[:], in_=d),
                           reads=(), writes=(key,))
                return RING[s], key

        ring = WRing()

        def mm_chain(ps_ap, pieces, reads, wkey):
            n = len(pieces)
            fns = [(lambda t, lh=lh, rh=rh, st=(j == 0), sp=(j == n - 1), o=ps_ap:
                    t.matmul(o, lh, rh, start=st, stop=sp))
                   for j, (lh, rh) in enumerate(pieces)]
            P.emit("pe", fns, reads=reads, writes=(wkey,))

        for c in range(DC):
            dma_sp(X[c][:], xin[c * 128:(c + 1) * 128, :], (),
                   tuple(xk(c, n) for n in range(NN)))
        dma_sp(PERM[:], permd, (), (("PERM",),))
        dma_sp(IDENT[:], identd, (), (("IDENT",),))
        dma_sp(ONES1[:], ones1d, (), (("ONES1",),))
        dma_sp(ONESB[:], onesbd, (), (("ONESB",),))
        dma_sp(ONESW[:], oneswd, (), (("ONESW",),))
        dma_sp(RBT[:], rbd, (), (("RBT",),))
        dma_gp(WRT[:], wrd, (), (("WRT",),))

        def emit_layernorm(eps, sc_col, bi_col, bias_key, out_fp32, ns=(0, 1, 2, 3)):
            def rowm(p):
                return SCR[1][0:1, p * 512:(p + 1) * 512]

            def rowa(p):
                return SCR[1][0:1, 1024 + p * 512: 1536 + p * 512]

            def rowv(p):
                return SMALL[0:1, 512 + p * 512: 1024 + p * 512]
            KM = lambda p: ("ROWM", p)
            KA = lambda p: ("ROWA", p)
            KV_ = lambda p: ("ROWV", p)
            lo, hi = min(ns) * NT, (max(ns) + 1) * NT
            for c in range(DC):
                P.emit("dve", lambda e, c=c, lo=lo, hi=hi: e.tensor_copy(
                    XBF[c][:, lo:hi], X[c][:, lo:hi]),
                    reads=[xk(c, n) for n in ns],
                    writes=[xbk(c, n) for n in ns])
                P.emit("act", lambda e, c=c, lo=lo, hi=hi: e.activation(
                    QF[c][:, lo:hi], XBF[c][:, lo:hi], AF.Square),
                    reads=[xbk(c, n) for n in ns],
                    writes=[qfk(c, n) for n in ns])

            def emit_stats(n):
                bs = 4 + 2 * (n % 2)
                p = n % 2
                mm_chain(PSB[bs][0:1, :],
                         [(ONESB[:], XBF[c][:, nsl(n)]) for c in range(DC)],
                         reads=[("ONESB",)] + [xbk(c, n) for c in range(DC)],
                         wkey=("PSB", bs))
                mm_chain(PSB[bs + 1][0:1, :],
                         [(ONESB[:], QF[c][:, nsl(n)]) for c in range(DC)],
                         reads=[("ONESB",)] + [qfk(c, n) for c in range(DC)],
                         wkey=("PSB", bs + 1))
                amk = [KM(p), sck(1, p)]
                aak = [KA(p), sck(1, 2 + p)]
                avk = [KV_(p)] + ([("CWFB",)] if p == 0 else [])
                P.emit("act", lambda e, bs=bs, p=p: e.activation(rowm(p), PSB[bs][0:1, :], AF.Identity, scale=np.float32(1.0 / D)),
                       reads=[("PSB", bs)], writes=amk)
                P.emit("act", lambda e, bs=bs, p=p: e.activation(rowv(p), PSB[bs + 1][0:1, :], AF.Identity, scale=np.float32(1.0 / D)),
                       reads=[("PSB", bs + 1)], writes=avk)
                P.emit("dve", lambda e, p=p: e.tensor_tensor(rowa(p), rowm(p), rowm(p), ALU.mult),
                       reads=amk, writes=aak)
                P.emit("dve", lambda e, p=p: e.tensor_tensor(rowv(p), rowv(p), rowa(p), ALU.subtract),
                       reads=avk + aak, writes=avk)
                P.emit("act", lambda e, eps=eps, p=p: e.activation(rowv(p), rowv(p), AF.Ln, bias=np.float32(eps)),
                       reads=avk, writes=avk)
                P.emit("act", lambda e, p=p: e.activation(rowa(p), rowv(p), AF.Exp, scale=np.float32(-0.5)),
                       reads=avk, writes=aak)
                P.emit("dve", lambda e, p=p: e.tensor_tensor(rowm(p), rowm(p), rowa(p), ALU.mult),
                       reads=amk + aak, writes=amk)

            def emit_bcast(n):
                bs = 4 + 2 * (n % 2)
                p = n % 2
                P.emit("pe", lambda t, bs=bs, p=p: t.matmul(PSB[bs][:, :], ONES1[:], rowa(p), start=True, stop=True),
                       reads=[("ONES1",), KA(p), sck(1, 2 + p)], writes=[("PSB", bs)])
                P.emit("pe", lambda t, bs=bs, p=p: t.matmul(PSB[bs + 1][:, :], ONES1[:], rowm(p), start=True, stop=True),
                       reads=[("ONES1",), KM(p), sck(1, p)], writes=[("PSB", bs + 1)])

            def emit_apply(n):
                bs = 4 + 2 * (n % 2)
                for c in range(DC):
                    P.emit("dve", lambda e, c=c, n=n, bs=bs: e.tensor_tensor(
                        SCR[0][:, nsl(n)], X[c][:, nsl(n)], PSB[bs][:, :], ALU.mult),
                        reads=[xk(c, n), ("PSB", bs)], writes=[sck(0, n)])
                    if sc_col is None:
                        if out_fp32:
                            P.emit("dve", lambda e, c=c, n=n, bs=bs: e.tensor_tensor(
                                X[c][:, nsl(n)], SCR[0][:, nsl(n)], PSB[bs + 1][:, :], ALU.subtract),
                                reads=[sck(0, n), ("PSB", bs + 1)], writes=[xk(c, n)])
                            P.emit("act", lambda e, c=c, n=n: e.activation(
                                XBF[c][:, nsl(n)], X[c][:, nsl(n)], AF.Identity),
                                reads=[xk(c, n)], writes=[xbk(c, n)])
                        else:
                            P.emit("dve", lambda e, c=c, n=n, bs=bs: e.tensor_tensor(
                                XBF[c][:, nsl(n)], SCR[0][:, nsl(n)], PSB[bs + 1][:, :], ALU.subtract),
                                reads=[sck(0, n), ("PSB", bs + 1)], writes=[xbk(c, n)])
                    else:
                        P.emit("dve", lambda e, c=c, n=n, bs=bs: e.tensor_tensor(
                            SCR[2][:, nsl(n)], SCR[0][:, nsl(n)], PSB[bs + 1][:, :], ALU.subtract),
                            reads=[sck(0, n), ("PSB", bs + 1)], writes=[sck(2, n)])
                        P.emit("act", lambda e, c=c, n=n: e.activation(
                            X[c][:, nsl(n)], SCR[2][:, nsl(n)], AF.Identity,
                            bias=bi_col(c), scale=sc_col(c)),
                            reads=[sck(2, n), bias_key], writes=[xk(c, n)])
                        P.emit("dve", lambda e, c=c, n=n: e.tensor_copy(
                            XBF[c][:, nsl(n)], X[c][:, nsl(n)]),
                            reads=[xk(c, n)], writes=[xbk(c, n)])

            sched = list(ns)
            for j, n in enumerate(sched):
                emit_stats(n)
                if j >= 1:
                    emit_bcast(sched[j - 1])
                    emit_apply(sched[j - 1])
            emit_bcast(sched[-1])
            emit_apply(sched[-1])

        def emit_attn_a_layer(i):
            dma_sp(CBT[:], cbd[i], (("CBc",),), (("CBc",),))

            def bcol(j):
                return CBT[:, j:j + 1]

            P.emit("dve", lambda e: e.tensor_scalar(NEGB[:], CBT[:, 0:16], -1.0, 0.0, ALU.mult),
                   reads=[("CBc",)], writes=[("NEGB",)])

            def nbcol(j):
                return NEGB[:, j:j + 1]

            def proj(wt, wkey, off, n, bank):
                mm_chain(PSB[bank][:, :],
                         [(wt[:, off + k * 128: off + (k + 1) * 128], XBF[k][:, nsl(n)]) for k in range(DC)],
                         reads=[wkey] + [xbk(k, n) for k in range(DC)],
                         wkey=("PSB", bank))

            def fmap(c, n, bank, bias_j, nbias_j, dst_ap, dst_keys, acc=False):
                accr = SMALL[:, S_PACKN + c * 8 + 4 + n: S_PACKN + c * 8 + 5 + n] if acc else None
                acce = SMALL[:, S_PACKN + c * 8 + n: S_PACKN + c * 8 + 1 + n] if acc else None
                P.emit("act", lambda e, n=n, accr=accr: e.activation(
                    SCR[0][:, nsl(n)], PSB[bank][:, :], AF.Relu, bias=bcol(bias_j),
                    scale=np.float32(1.0), accum_out=accr),
                    reads=[("PSB", bank), ("CBc",)],
                    writes=[sck(0, n)] + ([("PKN", 1, c, n)] if acc else []))
                P.emit("act", lambda e, n=n: e.activation(
                    SCR[1][:, nsl(n)], PSB[bank][:, :], AF.Relu, bias=nbcol(nbias_j), scale=np.float32(-1.0)),
                    reads=[("PSB", bank), ("NEGB",)], writes=[sck(1, n)])
                P.emit("act", lambda e, n=n, acce=acce: e.activation(
                    SCR[2][:, nsl(n)], SCR[1][:, nsl(n)], AF.Exp, scale=np.float32(-1.0),
                    accum_out=acce),
                    reads=[sck(1, n)],
                    writes=[sck(2, n)] + ([("PKN", 2, c, n)] if acc else []))
                P.emit("dve", lambda e, n=n, dst_ap=dst_ap: e.tensor_tensor(
                    dst_ap, SCR[0][:, nsl(n)], SCR[2][:, nsl(n)], ALU.add),
                    reads=[sck(0, n), sck(2, n)], writes=dst_keys)

            for c in range(DC):
                wt, wkey = ring.load(aw[i, c])
                for n in range(NN):
                    pg = n % 4
                    pv = 4 + n % 2
                    proj(wt, wkey, 0, n, pg)
                    proj(wt, wkey, 1024, n, pv)
                    fmap(c, n, pg, 8 + c, 8 + c, SCR[0][:, nsl(n)], [sck(0, n)], acc=True)
                    P.emit("act", lambda e, c=c, n=n, pv=pv: e.activation(
                        SCR[1][:, nsl(n)], PSB[pv][:, :], AF.Identity, bias=bcol(16 + c), scale=np.float32(1.0)),
                        reads=[("PSB", pv), ("CBc",)], writes=[sck(1, n)])
                    P.emit("dve", lambda e, n=n: e.tensor_tensor(
                        SCR[1][:, nsl(n)], SCR[0][:, nsl(n)], SCR[1][:, nsl(n)], ALU.mult),
                        reads=[sck(0, n), sck(1, n)], writes=[sck(1, n)])
                    P.emit("dve", lambda e, c=c, n=n: e.tensor_scalar(
                        SCR[2][:, nsl(n)], SCR[1][:, nsl(n)], 1.0, 0.0, ALU.mult, ALU.add,
                        accum_out=SMALL[:, S_PACKKV + c * 4 + n: S_PACKKV + c * 4 + n + 1]),
                        reads=[sck(1, n)], writes=[sck(2, n), ("PKN", 0, c, n)])
            for c in range(DC):
                P.emit("dve", lambda e, c=c: e.tensor_reduce(
                    SMALL[:, S_PACK + c: S_PACK + c + 1],
                    SMALL[:, S_PACKKV + c * 4: S_PACKKV + (c + 1) * 4], AXX, ALU.add),
                    reads=[("PKN", 0, c, n) for n in range(NN)], writes=[("PACK",)])
                P.emit("dve", lambda e, c=c: e.tensor_reduce(
                    SMALL[:, S_PACK + 8 + c: S_PACK + 9 + c],
                    SMALL[:, S_PACKN + c * 8: S_PACKN + (c + 1) * 8], AXX, ALU.add),
                    reads=[("PKN", 1, c, n) for n in range(NN)] + [("PKN", 2, c, n) for n in range(NN)],
                    writes=[("PACK",)])
            if i == 0:
                tap("pack", SMALL[:, S_PACK:S_PACK + 16], [("PACK",)])
            pre = [ring.load(aw[i, 8 + j]) for j in range(3)]
            dma_sp(cc1i.ap(), SMALL[:, S_PACK:S_PACK + 16], (("PACK",),), (("CC1I",),))
            P.emit("gp", lambda g: g.collective_compute(
                "AllReduce", ALU.add, replica_groups=[[0, 1], [2, 3], [4, 5], [6, 7]],
                ins=[cc1i.ap().opt()], outs=[cc1o.ap().opt()]),
                reads=[("CC1I",)], writes=[("CC1O",)], sem="s_gcc", inc=1)
            dma_sp(SMALL[:, S_KVS:S_KVS + 16], cc1o.ap(), (("CC1O",),), (("KVS",),))

            if i == 0:
                tap("kvs", SMALL[:, S_KVS:S_KVS + 16], [("KVS",)])
            def emit_q(H):
                for j in range(4):
                    if H == 0 and j < 3:
                        wt, wkey = pre[j]
                    else:
                        wt, wkey = ring.load(aw[i, 8 + j])
                    for ch in range(2):
                        c = 2 * j + ch
                        for n in (2 * H, 2 * H + 1):
                            pg = 2 + n % 2
                            proj(wt, wkey, ch * 1024, n, pg)
                            fmap(c, n, pg, c, c, QF[c][:, nsl(n)], [qfk(c, n)])

            def emit_att(H):
                for n in (2 * H, 2 * H + 1):
                    for c in range(DC):
                        P.emit("act", lambda e, c=c, n=n: e.activation(
                            SCR[0][:, nsl(n)], QF[c][:, nsl(n)], AF.Identity,
                            scale=SMALL[:, S_KVS + 8 + c: S_KVS + 9 + c], bias=np.float32(1e-6)),
                            reads=[qfk(c, n), ("KVS",)], writes=[sck(0, n)])
                        P.emit("dve", lambda e, n=n: e.reciprocal(SCR[1][:, nsl(n)], SCR[0][:, nsl(n)]),
                               reads=[sck(0, n)], writes=[sck(1, n)])
                        P.emit("act", lambda e, c=c, n=n: e.activation(
                            SCR[0][:, nsl(n)], QF[c][:, nsl(n)], AF.Identity,
                            scale=SMALL[:, S_KVS + c: S_KVS + c + 1]),
                            reads=[qfk(c, n), ("KVS",)], writes=[sck(0, n)])
                        P.emit("dve", lambda e, c=c, n=n: e.tensor_tensor(
                            XBF[c][:, nsl(n)], SCR[0][:, nsl(n)], SCR[1][:, nsl(n)], ALU.mult),
                            reads=[sck(0, n), sck(1, n)], writes=[xbk(c, n)])

            def emit_wo(H):
                for j in range(4):
                    wt, wkey = ring.load(aw[i, 12 + j])
                    for n in (2 * H, 2 * H + 1):
                        for ch in range(2):
                            m = 2 * j + ch
                            pg = 2 + (2 * n + ch) % 2
                            proj(wt, wkey, ch * 1024, n, pg)
                            P.emit("act", lambda e, m=m, n=n, pg=pg: e.activation(
                                SCR[2][:, nsl(n)], PSB[pg][:, :], AF.Gelu_apprx_tanh,
                                bias=bcol(24 + m), scale=np.float32(1.0)),
                                reads=[("PSB", pg), ("CBc",)], writes=[sck(2, n)])
                            P.emit("dve", lambda e, m=m, n=n: e.tensor_tensor(
                                X[m][:, nsl(n)], X[m][:, nsl(n)], SCR[2][:, nsl(n)], ALU.add),
                                reads=[xk(m, n), sck(2, n)], writes=[xk(m, n)])

            def hv(m):
                if m < 16:
                    return QF[m // 2][:, (m % 2) * 1024:(m % 2) * 1024 + 1024]
                j = m - 16
                return POOLB[:, (j // 2) * T + (j % 2) * 1024:(j // 2) * T + (j % 2) * 1024 + 1024]

            def hkeys(m, nn_):
                if m < 16:
                    return [qfk(m // 2, (m % 2) * 2 + nn_)]
                j = m - 16
                return [pbk(j // 2, (j % 2) * 2 + nn_)]

            def emit_ffn_half(h):
                for jm in range(16):
                    wt, wkey = ring.load(aw[i, 16 + jm])
                    for half in range(2):
                        m = 2 * jm + half
                        for nn_ in range(2):
                            n = 2 * h + nn_
                            pg = nn_
                            proj(wt, wkey, half * 1024, n, pg)
                            P.emit("act", lambda e, m=m, n=n, nn_=nn_, pg=pg: e.activation(
                                hv(m)[:, nsl(nn_)], PSB[pg][:, :], AF.Gelu_apprx_tanh,
                                bias=bcol(32 + m), scale=np.float32(1.0)),
                                reads=[("PSB", pg), ("CBc",)], writes=hkeys(m, nn_))
                for mo in range(DC):
                    wt0, wk0 = ring.load(aw[i, 32 + 2 * mo])
                    wt1, wk1 = ring.load(aw[i, 33 + 2 * mo])
                    for nn_ in range(2):
                        n = 2 * h + nn_
                        pg = nn_
                        pieces = [(wt0[:, k * 128:(k + 1) * 128], hv(k)[:, nsl(nn_)]) for k in range(16)]
                        pieces += [(wt1[:, (k - 16) * 128:(k - 15) * 128], hv(k)[:, nsl(nn_)]) for k in range(16, 32)]
                        rd = [wk0, wk1]
                        for k in range(32):
                            rd += hkeys(k, nn_)
                        mm_chain(PSB[pg][:, :], pieces, reads=rd, wkey=("PSB", pg))
                        P.emit("act", lambda e, mo=mo, n=n, pg=pg: e.activation(
                            SCR[2][:, nsl(n)], PSB[pg][:, :], AF.Gelu_apprx_tanh,
                            bias=bcol(64 + mo), scale=np.float32(1.0)),
                            reads=[("PSB", pg), ("CBc",)], writes=[sck(2, n)])
                        P.emit("dve", lambda e, mo=mo, n=n: e.tensor_tensor(
                            X[mo][:, nsl(n)], X[mo][:, nsl(n)], SCR[2][:, nsl(n)], ALU.add),
                            reads=[xk(mo, n), sck(2, n)], writes=[xk(mo, n)])

            emit_q(0)
            emit_q(1)
            emit_att(0)
            emit_att(1)
            emit_wo(0)
            emit_wo(1)
            emit_layernorm(EPS_IN, lambda c: bcol(72 + c), lambda c: bcol(80 + c),
                           ("CBc",), out_fp32=True, ns=(0, 1))
            emit_ffn_half(0)
            emit_layernorm(EPS_IN, lambda c: bcol(72 + c), lambda c: bcol(80 + c),
                           ("CBc",), out_fp32=True, ns=(2, 3))
            emit_layernorm(EPS_IN, lambda c: bcol(88 + c), lambda c: bcol(96 + c),
                           ("CBc",), out_fp32=True, ns=(0, 1))
            emit_ffn_half(1)
            emit_layernorm(EPS_IN, lambda c: bcol(88 + c), lambda c: bcol(96 + c),
                           ("CBc",), out_fp32=True, ns=(2, 3))

        def emit_ffn():
            for tt in range(16):
                pieces = [(XBF[k][:, tt * 128:(tt + 1) * 128], WRT[:, k * 8:(k + 1) * 8])
                          for k in range(DC)]
                mm_chain(PSB[0][:, 0:8], pieces,
                         reads=[("WRT",)] + [xbk(k, tt // 4) for k in range(DC)],
                         wkey=("PSB", 0))
                P.emit("dve", lambda e: e.tensor_tensor(
                    SMALL[:, S_JNK:S_JNK + 8], PSB[0][:, 0:8], RBT[:], ALU.add),
                    reads=[("PSB", 0), ("RBT",)], writes=[("JNK",)])
                P.emit("act", lambda e, tt=tt: e.activation(
                    SMALL[:, S_PRT + tt * 8:S_PRT + (tt + 1) * 8],
                    SMALL[:, S_JNK:S_JNK + 8], AF.Sigmoid),
                    reads=[("JNK",)], writes=[("PRT", tt)])
            for tt in range(16):
                prt = SMALL[:, S_PRT + tt * 8:S_PRT + (tt + 1) * 8]
                m0 = SMALL[:, S_M0 + tt:S_M0 + tt + 1]
                m1 = SMALL[:, S_M1 + tt:S_M1 + tt + 1]
                c0 = SMALL[:, S_C0 + tt:S_C0 + tt + 1]
                f1 = SMALL[:, S_F1 + tt:S_F1 + tt + 1]
                thr = SMALL[:, S_THR + tt:S_THR + tt + 1]
                jnk = SMALL[:, S_JNK:S_JNK + 8]
                cwt = SMALL[:, S_CWT + tt * 8:S_CWT + (tt + 1) * 8]
                P.emit("dve", lambda e, prt=prt, m0=m0: e.tensor_reduce(m0, prt, AXX, ALU.max),
                       reads=[("PRT", tt)], writes=[("M0", tt)])
                P.emit("dve", lambda e, prt=prt, m0=m0, jnk=jnk, c0=c0: e.tensor_scalar(
                    jnk, prt, m0, 0.0, ALU.is_ge, ALU.add, accum_out=c0),
                    reads=[("PRT", tt), ("M0", tt)], writes=[("JNK",), ("C0", tt)])
                P.emit("dve", lambda e, jnk=jnk: e.tensor_scalar(jnk, jnk, -1e30, 0.0, ALU.mult),
                       reads=[("JNK",)], writes=[("JNK",)])
                P.emit("dve", lambda e, prt=prt, jnk=jnk: e.tensor_tensor(jnk, jnk, prt, ALU.add),
                       reads=[("JNK",), ("PRT", tt)], writes=[("JNK",)])
                P.emit("dve", lambda e, jnk=jnk, m1=m1: e.tensor_reduce(m1, jnk, AXX, ALU.max),
                       reads=[("JNK",)], writes=[("M1", tt)])
                P.emit("dve", lambda e, c0=c0, f1=f1: e.tensor_scalar(f1, c0, 1.0, 0.0, ALU.is_le),
                       reads=[("C0", tt)], writes=[("F1", tt)])
                P.emit("dve", lambda e, m1=m1, m0=m0, thr=thr: e.tensor_tensor(thr, m1, m0, ALU.subtract),
                       reads=[("M1", tt), ("M0", tt)], writes=[("THR", tt)])
                P.emit("dve", lambda e, thr=thr, f1=f1: e.tensor_tensor(thr, thr, f1, ALU.mult),
                       reads=[("THR", tt), ("F1", tt)], writes=[("THR", tt)])
                P.emit("dve", lambda e, thr=thr, m0=m0: e.tensor_tensor(thr, thr, m0, ALU.add),
                       reads=[("THR", tt), ("M0", tt)], writes=[("THR", tt)])
                P.emit("dve", lambda e, prt=prt, thr=thr, cwt=cwt: e.tensor_scalar(
                    cwt, prt, thr, 0.0, ALU.is_ge),
                    reads=[("PRT", tt), ("THR", tt)], writes=[("CWT", tt)])
                P.emit("dve", lambda e, cwt=cwt, prt=prt: e.tensor_tensor(cwt, cwt, prt, ALU.mult),
                       reads=[("CWT", tt), ("PRT", tt)], writes=[("CWT", tt)])

            for n in range(NN):
                for s4 in range(4):
                    tt = n * 4 + s4
                    P.emit("pe", lambda t, tt=tt, s4=s4: t.transpose(
                        PSB[7][0:8, s4 * 128:(s4 + 1) * 128],
                        SMALL[:, S_CWT + tt * 8:S_CWT + (tt + 1) * 8], IDENT[:]),
                        reads=[("CWT", tt), ("IDENT",)],
                        writes=[("PSB", 7)] if s4 == 3 else [("PSB7p", s4)])
                P.emit("dve", lambda e: e.tensor_copy(CWFB, PSB[7][0:8, :]),
                       reads=[("PSB", 7)] + [("PSB7p", s4) for s4 in range(3)],
                       writes=[("CWFB",), ("ROWM",)])
                dma_sp(cwrows.ap()[:, nsl(n)], CWFB, (("CWFB",),), (("CWROWS", n),))
            for ex in range(9):
                def hv2(m):
                    return QF[m] if m < 8 else pb(m - 8)

                def hk2(m, n):
                    return qfk(m, n) if m < 8 else pbk(m - 8, n)
                for m in range(16):
                    wt, wkey = ring.load(fw[ex, m])
                    for n in range(NN):
                        pg = n % 2
                        pu = 4 + n % 2
                        mm_chain(PSB[pg][:, :],
                                 [(wt[:, k * 128:(k + 1) * 128], XBF[k][:, nsl(n)]) for k in range(DC)],
                                 reads=[wkey] + [xbk(k, n) for k in range(DC)],
                                 wkey=("PSB", pg))
                        mm_chain(PSB[pu][:, :],
                                 [(wt[:, 1024 + k * 128:1024 + (k + 1) * 128], XBF[k][:, nsl(n)]) for k in range(DC)],
                                 reads=[wkey] + [xbk(k, n) for k in range(DC)],
                                 wkey=("PSB", pu))
                        P.emit("act", lambda e, n=n, pg=pg: e.activation(
                            SCR[2][:, nsl(n)], PSB[pg][:, :], AF.Silu),
                            reads=[("PSB", pg)], writes=[sck(2, n)])
                        P.emit("dve", lambda e, m=m, n=n, pu=pu: e.tensor_tensor(
                            hv2(m)[:, nsl(n)], SCR[2][:, nsl(n)], PSB[pu][:, :], ALU.mult),
                            reads=[sck(2, n), ("PSB", pu)], writes=[hk2(m, n)])
                for mo in range(DC):
                    wt, wkey = ring.load(fw[ex, 16 + mo])
                    for n in range(NN):
                        pg = n % 2
                        rd = [wkey] + [hk2(k, n) for k in range(16)]
                        mm_chain(PSB[pg][:, :],
                                 [(wt[:, k * 128:(k + 1) * 128], hv2(k)[:, nsl(n)]) for k in range(16)],
                                 reads=rd, wkey=("PSB", pg))
                        if ex == 0:
                            P.emit("dve", lambda e, mo=mo, n=n, pg=pg: e.tensor_tensor(
                                X[mo][:, nsl(n)], X[mo][:, nsl(n)], PSB[pg][:, :], ALU.add),
                                reads=[xk(mo, n), ("PSB", pg)], writes=[xk(mo, n)])
                        else:
                            if mo == 0:
                                srow = cwrows.ap()[ex - 1:ex, nsl(n)].squeeze(0).partition_broadcast(128)
                                dma_sp(SCR[2][:, nsl(n)], srow, (("CWROWS", n),), (sck(2, n),))
                            P.emit("dve", lambda e, n=n, pg=pg: e.tensor_tensor(
                                SCR[1][:, nsl(n)], PSB[pg][:, :], SCR[2][:, nsl(n)], ALU.mult),
                                reads=[("PSB", pg), sck(2, n)], writes=[sck(1, n)])
                            P.emit("dve", lambda e, mo=mo, n=n: e.tensor_tensor(
                                X[mo][:, nsl(n)], X[mo][:, nsl(n)], SCR[1][:, nsl(n)], ALU.add),
                                reads=[xk(mo, n), sck(1, n)], writes=[xk(mo, n)])

        def emit_attn_b():
            dma_gp(pb(0), cosd, (), tuple(pbk(0, q) for q in range(NN)))
            dma_gp(pb(1), sind, (), tuple(pbk(1, q) for q in range(NN)))
            def kftv(s):
                return POOLB[:, (2 + s // 2) * T + (s % 2) * 1024: (2 + s // 2) * T + (s % 2) * 1024 + 1024]

            def kftkeys(s):
                return [pbk(2 + s // 2, (s % 2) * 2), pbk(2 + s // 2, (s % 2) * 2 + 1)]

            def vtv(s):
                return POOLB[:, (4 + s // 2) * T + (s % 2) * 1024: (4 + s // 2) * T + (s % 2) * 1024 + 1024]

            def vtkeys(s):
                return [pbk(4 + s // 2, (s % 2) * 2), pbk(4 + s // 2, (s % 2) * 2 + 1)]

            for g in range(NN):
                for j in range(12):
                    wt, wkey = ring.load(bw[j])
                    for half in range(2):
                        cc = 2 * j + half
                        pg = cc % 2
                        mm_chain(PSB[pg][:, :],
                                 [(wt[:, half * 1024 + k * 128: half * 1024 + (k + 1) * 128],
                                   XBF[k][:, nsl(g)]) for k in range(DC)],
                                 reads=[wkey] + [xbk(k, g) for k in range(DC)],
                                 wkey=("PSB", pg))
                        if cc < 16:
                            c = cc % 8
                            isq = cc < 8
                            P.emit("act", lambda e, pg=pg, g=g: e.activation(
                                SCR[0][:, nsl(g)], PSB[pg][:, :], AF.Identity),
                                reads=[("PSB", pg)], writes=[sck(0, g)])
                            P.emit("pe", lambda t, g=g: t.matmul(
                                PSB[2][:, :], PERM[:], SCR[0][:, nsl(g)], start=True, stop=True),
                                reads=[("PERM",), sck(0, g)], writes=[("PSB", 2)])
                            P.emit("dve", lambda e, g=g: e.tensor_tensor(
                                SCR[1][:, nsl(g)], SCR[0][:, nsl(g)], pb(0)[:, nsl(g)], ALU.mult),
                                reads=[sck(0, g), pbk(0, g)], writes=[sck(1, g)])
                            P.emit("dve", lambda e, g=g: e.tensor_tensor(
                                SCR[0][:, nsl(g)], PSB[2][:, :], pb(1)[:, nsl(g)], ALU.mult),
                                reads=[("PSB", 2), pbk(1, g)], writes=[sck(0, g)])
                            P.emit("dve", lambda e, g=g: e.tensor_tensor(
                                SCR[0][:, nsl(g)], SCR[0][:, nsl(g)], SCR[1][:, nsl(g)], ALU.add),
                                reads=[sck(0, g), sck(1, g)], writes=[sck(0, g)])
                            sc = SCALE_B if isq else 1.0
                            P.emit("act", lambda e, g=g, sc=sc: e.activation(
                                SCR[1][:, nsl(g)], SCR[0][:, nsl(g)], AF.Relu, scale=np.float32(sc)),
                                reads=[sck(0, g)], writes=[sck(1, g)])
                            P.emit("act", lambda e, g=g, sc=sc: e.activation(
                                SCR[2][:, nsl(g)], SCR[0][:, nsl(g)], AF.Relu, scale=np.float32(-sc)),
                                reads=[sck(0, g)], writes=[sck(2, g)])
                            P.emit("act", lambda e, g=g: e.activation(
                                SCR[2][:, nsl(g)], SCR[2][:, nsl(g)], AF.Exp, scale=np.float32(-1.0)),
                                reads=[sck(2, g)], writes=[sck(2, g)])
                            if isq:
                                P.emit("dve", lambda e, c=c, g=g: e.tensor_tensor(
                                    QF[c][:, nsl(g)], SCR[2][:, nsl(g)], SCR[1][:, nsl(g)], ALU.add),
                                    reads=[sck(2, g), sck(1, g)], writes=[qfk(c, g)])
                            else:
                                P.emit("dve", lambda e, g=g: e.tensor_tensor(
                                    SCR[0][:, nsl(g)], SCR[2][:, nsl(g)], SCR[1][:, nsl(g)], ALU.add),
                                    reads=[sck(2, g), sck(1, g)], writes=[sck(0, g)])
                                P.emit("act", lambda e, c=c, g=g: e.activation(
                                    SCR[1][:, nsl(g)], SCR[0][:, nsl(g)], AF.Identity,
                                    accum_out=SMALL[:, S_KSN + c * 4 + g: S_KSN + c * 4 + g + 1]),
                                    reads=[sck(0, g)], writes=[sck(1, g), ("KSN", c, g)])
                                for s in range(4):
                                    P.emit("pe", lambda t, g=g, s=s: t.transpose(
                                        PSB[3][:, 0:128],
                                        SCR[0][:, g * NT + s * 128: g * NT + (s + 1) * 128], IDENT[:]),
                                        reads=[sck(0, g), ("IDENT",)], writes=[("PSB", 3)])
                                    P.emit("dve", lambda e, c=c, s=s: e.tensor_copy(
                                        kftv(s)[:, c * 128:(c + 1) * 128], PSB[3][:, 0:128]),
                                        reads=[("PSB", 3)], writes=kftkeys(s))
                        else:
                            c = cc - 16
                            P.emit("act", lambda e, pg=pg, g=g: e.activation(
                                SCR[0][:, nsl(g)], PSB[pg][:, :], AF.Identity),
                                reads=[("PSB", pg)], writes=[sck(0, g)])
                            for s in range(4):
                                P.emit("pe", lambda t, g=g, s=s: t.transpose(
                                    PSB[3][:, 0:128],
                                    SCR[0][:, g * NT + s * 128: g * NT + (s + 1) * 128], IDENT[:]),
                                    reads=[sck(0, g), ("IDENT",)], writes=[("PSB", 3)])
                                P.emit("dve", lambda e, c=c, s=s: e.tensor_copy(
                                    vtv(s)[:, c * 128:(c + 1) * 128], PSB[3][:, 0:128]),
                                    reads=[("PSB", 3)], writes=vtkeys(s))
                for lohi in range(2):
                    fns = []
                    tgt = PSB[4 + lohi]
                    for h in range(8):
                        hh = lohi * 8 + h
                        for s in range(4):
                            kft = kftv(s)[:, hh * 64:(hh + 1) * 64]
                            vt = vtv(s)[:, hh * 64:(hh + 1) * 64]
                            st = (g == 0 and h == 0 and s == 0)
                            sp = (g == NN - 1 and h == 7 and s == 3)
                            fns.append(lambda t, kft=kft, vt=vt, tgt=tgt, col=h * 64, st=st, sp=sp:
                                       t.matmul(tgt[0:64, col:col + 64], kft, vt, start=st, stop=sp))
                    rd = []
                    for s in range(4):
                        rd += kftkeys(s) + vtkeys(s)
                    P.emit("pe", fns, reads=rd, writes=[("PSB", 4 + lohi)])
            for c in range(DC):
                P.emit("dve", lambda e, c=c: e.tensor_reduce(
                    SMALL[:, S_KS + c: S_KS + c + 1],
                    SMALL[:, S_KSN + c * 4: S_KSN + (c + 1) * 4], AXX, ALU.add),
                    reads=[("KSN", c, g) for g in range(NN)], writes=[("KS",)])
            P.emit("dve", lambda e: e.tensor_copy(SCR[0][0:64, 0:NT], PSB[4][0:64, :]),
                   reads=[("PSB", 4)], writes=[sck(0, 0)])
            P.emit("dve", lambda e: e.tensor_copy(SCR[0][0:64, NT:2 * NT], PSB[5][0:64, :]),
                   reads=[("PSB", 5)], writes=[sck(0, 1)])
            dma_sp(cc2i.ap()[:, 0:1024], SCR[0][0:64, 0:1024], (sck(0, 0), sck(0, 1)), (("CC2I", 0),))
            dma_sp(cc2i.ap()[:, 1024:1032], SMALL[0:64, S_KS:S_KS + 8], (("KS",),), (("CC2I", 1),))
            dma_sp(cc2i.ap()[:, 1032:1040], SMALL[64:128, S_KS:S_KS + 8], (("KS",),), (("CC2I", 2),))
            P.emit("gp", lambda g: g.collective_compute(
                "AllReduce", ALU.add, replica_groups=[[0, 1], [2, 3], [4, 5], [6, 7]],
                ins=[cc2i.ap().opt()], outs=[cc2o.ap().opt()]),
                reads=[("CC2I", 0), ("CC2I", 1), ("CC2I", 2)], writes=[("CC2O",)], sem="s_gcc", inc=1)
            dma_sp(KS2B[0:64, :], cc2o.ap()[:, 1024:1032], (("CC2O",),), (("KS2B",),))
            dma_sp(KS2B[64:128, :], cc2o.ap()[:, 1032:1040], (("CC2O",),), (("KS2B",),))
            for c in range(DC):
                kvl = pb(2)[:, c * 128:(c + 1) * 128]
                ksl = pb(3)[:, c * 128:(c + 1) * 128]
                P.emit("gp", lambda g, kvl=kvl: g.memset(kvl, 0.0),
                       reads=(), writes=[("KVL", c), pbk(2, (c * 128) // NT)],
                       sem="s_gpo", inc=1)
                dma_gp(kvl[0:64, 0:64], cc2o.ap()[0:64, (2 * c) * 64:(2 * c + 1) * 64],
                       (("CC2O",), ("KVL", c)), (("KVL", c),))
                dma_gp(kvl[64:128, 64:128], cc2o.ap()[0:64, (2 * c + 1) * 64:(2 * c + 2) * 64],
                       (("CC2O",), ("KVL", c)), (("KVL", c),))
                P.emit("gp", lambda g, ksl=ksl: g.memset(ksl, 0.0),
                       reads=(), writes=[("KSL", c), pbk(3, (c * 128) // NT)],
                       sem="s_gpo", inc=1)
                P.emit("dve", lambda e, ksl=ksl, c=c: e.tensor_scalar(
                    ksl[0:64, 0:64], ONESW[0:64, :], KS2B[0:64, c:c + 1], 0.0, ALU.mult),
                    reads=[("ONESW",), ("KS2B",), ("KSL", c)], writes=[("KSL", c)])
                P.emit("dve", lambda e, ksl=ksl, c=c: e.tensor_scalar(
                    ksl[64:128, 64:128], ONESW[64:128, :], KS2B[64:128, c:c + 1], 0.0, ALU.mult),
                    reads=[("ONESW",), ("KS2B",), ("KSL", c)], writes=[("KSL", c)])
            for n in range(NN):
                for c in range(DC):
                    kvl = pb(2)[:, c * 128:(c + 1) * 128]
                    ksl = pb(3)[:, c * 128:(c + 1) * 128]
                    pg = n % 2
                    P.emit("pe", lambda t, c=c, n=n, kvl=kvl, pg=pg: t.matmul(
                        PSB[pg][:, :], kvl, QF[c][:, nsl(n)], start=True, stop=True),
                        reads=[("KVL", c), qfk(c, n)], writes=[("PSB", pg)])
                    P.emit("pe", lambda t, c=c, n=n, ksl=ksl: t.matmul(
                        PSB[2][:, :], ksl, QF[c][:, nsl(n)], start=True, stop=True),
                        reads=[("KSL", c), qfk(c, n)], writes=[("PSB", 2)])
                    P.emit("dve", lambda e, n=n: e.tensor_scalar(
                        SCR[0][:, nsl(n)], PSB[2][:, :], 1e-6, 0.0, ALU.max),
                        reads=[("PSB", 2)], writes=[sck(0, n)])
                    P.emit("dve", lambda e, n=n: e.reciprocal(SCR[1][:, nsl(n)], SCR[0][:, nsl(n)]),
                           reads=[sck(0, n)], writes=[sck(1, n)])
                    P.emit("dve", lambda e, c=c, n=n, pg=pg: e.tensor_tensor(
                        XBF[c][:, nsl(n)], PSB[pg][:, :], SCR[1][:, nsl(n)], ALU.mult),
                        reads=[("PSB", pg), sck(1, n)], writes=[xbk(c, n)])
            for j in range(4):
                wt, wkey = ring.load(bw[12 + j])
                for half in range(2):
                    mo = 2 * j + half
                    for n in range(NN):
                        pg = n % 2
                        mm_chain(PSB[pg][:, :],
                                 [(wt[:, half * 1024 + k * 128: half * 1024 + (k + 1) * 128],
                                   XBF[k][:, nsl(n)]) for k in range(DC)],
                                 reads=[wkey] + [xbk(k, n) for k in range(DC)],
                                 wkey=("PSB", pg))
                        P.emit("dve", lambda e, mo=mo, n=n, pg=pg: e.tensor_tensor(
                            X[mo][:, nsl(n)], X[mo][:, nsl(n)], PSB[pg][:, :], ALU.add),
                            reads=[xk(mo, n), ("PSB", pg)], writes=[xk(mo, n)])

        def phase_res1():
            for c in range(DC):
                j = c % 3
                P.emit_dma("sp", lambda s, c=c, j=j: s.dma_start(
                    out=SCR[j][:], in_=xin[c * 128:(c + 1) * 128, :]),
                    reads=[sck(j, n) for n in range(NN)],
                    writes=[sck(j, n) for n in range(NN)])
                P.emit("dve", lambda e, c=c, j=j: e.tensor_tensor(X[c][:], X[c][:], SCR[j][:], ALU.add),
                       reads=[xk(c, n) for n in range(NN)] + [sck(j, n) for n in range(NN)],
                       writes=[xk(c, n) for n in range(NN)])

        phases = [("oln1", lambda: emit_layernorm(EPS_OUT, None, None, None, out_fp32=True))]
        for i in range(NLAYER):
            phases.append((f"al{i}", lambda i=i: emit_attn_a_layer(i)))
        phases += [
            ("res1", phase_res1),
            ("oln2", lambda: (P.barrier(), emit_layernorm(EPS_OUT, None, None, None, out_fp32=False), P.barrier())),
            ("ffn1", emit_ffn),
            ("oln3", lambda: (P.barrier(), emit_layernorm(EPS_OUT, None, None, None, out_fp32=False), P.barrier())),
            ("attnb", emit_attn_b),
            ("oln4", lambda: (P.barrier(), emit_layernorm(EPS_OUT, None, None, None, out_fp32=False), P.barrier())),
            ("ffn2", emit_ffn),
        ]
        for name, fn in phases:
            fn()
            P.flush()
            if name == upto:
                break
        for c in range(DC):
            P.emit_dma("sp", lambda s, c=c: s.dma_start(out=outd[c * 128:(c + 1) * 128, :], in_=X[c][:]),
                       reads=[xk(c, n) for n in range(NN)], writes=[("OUT", c)])
        P.wait_only("sp", [("OUT", c) for c in range(DC)])
        P.flush()

    return nc



def _prep_inputs(inputs):
    inp = {k: np.asarray(v) for k, v in inputs.items()}
    aw, cb = _pack_attn_a(inp)
    fwp = _pack_ffn(inp)
    bwp = _pack_attn_b(inp)
    WrT = np.ascontiguousarray(inp["L0_ffn_Wr"].T)
    wr_t = np.ascontiguousarray(WrT.reshape(DC, 128, 8).transpose(1, 0, 2).reshape(128, 64))
    rbt = np.ascontiguousarray(np.broadcast_to(inp["L0_ffn_rb"].astype(np.float32), (128, 8)))
    x = inp["L0_attn_a_x"]
    in_maps = []
    for core in range(8):
        seq, h = core // 2, core % 2
        xs = np.ascontiguousarray(x[seq, h * T:(h + 1) * T, :].T)
        m = dict(x=xs, aw=aw, fw=fwp, bw=bwp, cb=cb, wr=wr_t, rbt=rbt)
        m.update(_host_consts(core))
        in_maps.append(m)
    return x, in_maps


def kernel(**inputs):
    x, in_maps = _prep_inputs(inputs)
    nc = build_program()
    res = run_bass_kernel_spmd(nc, in_maps, core_ids=list(range(8)))
    out = np.empty_like(x)
    for core in range(8):
        seq, h = core // 2, core % 2
        out[seq, h * T:(h + 1) * T, :] = res.results[core]["out"].T
    return out
